# version 01
# baseline (speedup 1.0000x reference)
import contextlib
import numpy as np
import ml_dtypes
import concourse.bass as bass
import concourse.mybir as mybir
from concourse.bass_utils import run_bass_kernel_spmd

F32 = mybir.dt.float32
BF16 = mybir.dt.bfloat16
AF = mybir.ActivationFunctionType
ALU = mybir.AluOpType

D = 4096
ET = 18560
INTER = 8192
CONVD = 10240
NH, PH, NS, G = 128, 64, 128, 8
T = 1024
TH = 1027
NZ = 8
NEG_BIG = -1e30

ETS = [(18432, "dt")] + [(8192 + a * 128, "hbc") for a in range(80)] \
    + [(a * 128, "gate") for a in range(64)]
NET = len(ETS)
JC = [(0, 343), (343, 685), (685, 1027)]

ALL_PHASES = ["p0", "a", "e", "q0", "qa", "qb", "b2", "c", "d"]


def build_program(upto="d", dumps=()):
    nph = ALL_PHASES.index(upto) + 1
    use = ALL_PHASES[:nph]
    nc = bass.Bass("TRN2", target_bir_lowering=False, debug=False)

    def di(name, shape, dt=F32):
        return nc.dram_tensor(name, list(shape), dt, kind="ExternalInput").ap()

    xT = di("xT", (D, TH))
    xT2 = di("xT2", (D, TH))
    WinT = di("WinT", (D, ET))
    WoutT = di("WoutT", (INTER, D))
    kern = di("kern", (D, D))
    loraAt4 = di("loraAt4", (4, D, 128))
    loraBperm = di("loraBperm", (512, D))
    loraAst = di("loraAst", (D, 32))
    loraBst = di("loraBst", (32, D))
    gate_col = di("gate_col", (16, 1))
    conv_w4 = di("conv_w4", (128, 320))
    conv_b = di("conv_b", (128, 80))
    dtb_i = di("dtb", (NH, 1))
    Alog_i = di("Alog", (NH, 1))
    Dexp_i = di("Dexp", (128, 64))
    nwcol_i = di("nwcol", (128, 64))
    bias_i = di("bias_col", (128, 32))
    cmask_i = di("cmask", (NH, 1))
    umask_i = di("umask", (NH, 1))
    c_ident = di("c_ident", (128, 128), BF16)
    c_identf = di("c_identf", (128, 128), F32)
    c_sl = di("c_sl", (128, 128), BF16)
    c_negid = di("c_negid", (128, 128), BF16)
    c_bi16 = di("c_bi16", (48, 2048), BF16)
    c_ones = di("c_ones", (128, 128), BF16)
    c_id4 = di("c_id4", (128, 32), BF16)
    c_selg = di("c_selg", (16, 512), BF16)

    o_out = nc.dram_tensor("o_out", [T, D], F32, kind="ExternalOutput").ap()
    gate_scr = nc.dram_tensor("gate_scr", [INTER, T], BF16, kind="Internal").ap()
    hbc_scr = nc.dram_tensor("hbc_scr", [CONVD, T], BF16, kind="Internal").ap()
    y_scr = nc.dram_tensor("y_scr", [INTER, T], BF16, kind="Internal").ap()
    ffn_scr = nc.dram_tensor("ffn_scr", [D, T], F32, kind="Internal").ap()
    hbc2_scr = nc.dram_tensor("hbc2_scr", [9216, T], BF16, kind="Internal").ap()
    dt_scr = nc.dram_tensor("dt_scr", [896, T], BF16, kind="Internal").ap()

    DUMP_SPECS = {
        "xn": ((128, 32 * TH), BF16), "gate": ((INTER, T), BF16),
        "hbc": ((CONVD, T), BF16), "dtpack": ((128, 3072), F32),
        "dtbf": ((128, 4096), BF16),
        "y": ((INTER, T), BF16), "ffn": ((D, T), F32),
        "normed": ((128, 65536), BF16), "rstate": ((128, INTER), BF16),
        "vecs": ((128, 8), F32), "fm": ((128, 1024), F32),
        "ccout": ((128, INTER), F32), "axc": ((64, T), BF16),
        "bcat": ((64, D), BF16),
    }
    dump_aps = {}
    for name in dumps:
        shp, dt = DUMP_SPECS[name]
        dump_aps[name] = nc.dram_tensor("dump_" + name, list(shp), dt,
                                        kind="ExternalOutput").ap()

    with contextlib.ExitStack() as stx:
        E_ = stx.enter_context
        ar1 = E_(nc.sbuf_tensor("ar1", [128, 65536], BF16))
        arw = E_(nc.sbuf_tensor("arw", [128, 12288], BF16))
        arb = E_(nc.sbuf_tensor("arb", [128, 16384], BF16))
        arf = E_(nc.sbuf_tensor("arf", [128, 4352], F32))
        csm = E_(nc.sbuf_tensor("csm", [128, 512], BF16))
        ddsb = E_(nc.sbuf_tensor("ddsb", [128, 128], BF16))
        ssb = E_(nc.sbuf_tensor("ssb", [128, 128], BF16))
        gts = E_(nc.sbuf_tensor("gts", [8, 128], BF16))
        csg = E_(nc.sbuf_tensor("csg", [128, 8], F32))
        gbf = E_(nc.sbuf_tensor("gbf", [128, 8], BF16))
        cwsb = E_(nc.sbuf_tensor("cwsb", [128, 320], F32))
        cbsb = E_(nc.sbuf_tensor("cbsb", [128, 80], F32))
        vecs = E_(nc.sbuf_tensor("vecs", [128, 8], F32))
        dxsb = E_(nc.sbuf_tensor("dxsb", [128, 64], F32))
        nwsb = E_(nc.sbuf_tensor("nwsb", [128, 64], F32))
        bcsb = E_(nc.sbuf_tensor("bcsb", [128, 32], F32))
        gcsb = E_(nc.sbuf_tensor("gcsb", [16, 1], BF16))
        selg = E_(nc.sbuf_tensor("selg", [16, 512], BF16))
        i4sb = E_(nc.sbuf_tensor("i4sb", [128, 32], BF16))
        gcolf = E_(nc.sbuf_tensor("gcolf", [128, 4], F32))
        gdsb = E_(nc.sbuf_tensor("gdsb", [128, 128], BF16))
        zrf = E_(nc.sbuf_tensor("zrf", [128, 128], F32))
        kbias = E_(nc.sbuf_tensor("kbias", [128, 2], F32))
        gbc_sb = E_(nc.sbuf_tensor("gbc_sb", [128, 128], BF16))

        bi16 = arw[0:48, 9216:11264]
        bfc = ar1[:, 41056:45152]
        gamr = ar1[0:1, 46176:47200]
        gamr2 = ar1[0:1, 47200:48224]
        rowsa = ar1[0:3, 28672:30720]
        rowse = ar1[0:1, 30720:32768]
        ddec2 = arb[:, 12288:13312]
        stat = arf[0:1, 0:3081]
        statb = arf[0:1, 3081:4108].bitcast(BF16)
        asT = arb[:, 5120:6144]
        btsb = ar1[:, 45152:46176]
        idf = arf[:, 4224:4352]
        ard = arb[:, 8192:12288]

        xn = ar1[:, 0:32 * TH].rearrange("p (kb t) -> p kb t", kb=32)
        hsT = [ar1[:, 0:8192], ar1[:, 8192:16384]]
        l_sb = ar1[:, 16384:18432]
        ce_sb = ar1[:, 18432:20480]
        yst = ar1[:, 20480:28672]
        xchd = ar1[:, 32864:41056]
        b1hsT = [arb[:, 0:8192], arb[:, 0:8192]]
        r_f = ar1[:, 49152:57344]
        normed = ar1[:, 0:65536].rearrange("p (j t) -> p j t", j=64)

        wring = [arw[:, i * 4096:(i + 1) * 4096] for i in range(3)]
        w6 = [arw[:, i * 2048:(i + 1) * 2048] for i in range(6)]
        colvst = arw[0:48, 8192:9216]

        xf_st = arb[:, 0:2054].bitcast(F32)
        cvst = [arb[:, 2054 + i * 1027:2054 + (i + 1) * 1027] for i in range(2)]
        cvo = [arb[:, 4108 + i * 1024:4108 + (i + 1) * 1024] for i in range(2)]
        gst = [arb[:, 12288 + i * 1024:12288 + (i + 1) * 1024] for i in range(2)]
        sq_st = arb[:, 6156:7183]
        c_y = arb[:, 0:4096]
        c_g = arb[:, 4096:8192]
        c_h = arb[:, 8192:12288]
        c_w = arb[:, 12288:16384]
        e_bcat = arb[:, 0:4096]
        e_axc = arb[:, 4096:5120]
        e_axall = ar1[:, 32864:36960].rearrange("p (g t) -> p g t", g=4)

        f_dtr = arf[:, 0:1027]
        f_e = arf[:, 1027:2054]
        f_m = arf[:, 2054:3078]
        f_dt = arf[:, 3078:4102]
        f_ac = arf[:, 0:1024]
        d_xc = arf[:, 0:1024]
        d_fc = arf[:, 1024:2048]
        d_os = arf[:, 2048:3072]
        d_ot = arf[:, 3072:4096]
        e_st = [arf[:, 0:1024], arf[:, 1024:2048]]

        ard0 = ard[:, 0:1024]
        ard1 = ard[:, 1024:2048]
        b_expa = ard[:, 2048:3072]
        b_ddec = ard[:, 3072:4096]

        k_id = csm[:, 0:128]
        k_sl = csm[:, 128:256]
        k_negid = csm[:, 256:384]
        k_ones = csm[:, 384:512]

        pb = [E_(nc.psum_tensor(f"pb{i}", [128, 512], F32)) for i in range(6)]
        pbt = E_(nc.psum_tensor("pbt", [128, 1024], BF16))
        pbs = E_(nc.psum_tensor("pbs", [128, 512], F32))

        sem_names = [
            "bar",
            "p0x", "p0b", "p0m", "p0s", "p0c", "kc", "cst",
            "aw", "amm", "apc", "acnv", "aslu", "ast", "adt", "adtv",
            "agt", "age", "agr",
            "b1l", "b1t", "b1e", "b1x", "b1s", "b1r", "b1g", "b1gv", "b1c", "ccd",
            "q0x", "q0b", "q0m", "q0s", "q0c",
            "qw", "qmm", "qpc", "qcnv", "qslu", "qst", "qdt", "qdtv",
            "qgt", "qge", "qgr",
            "elt", "elb", "ele", "elv", "elp", "ew", "emm", "eev", "est",
            "b2l", "b2t", "b2e", "b2x", "b2s", "b2r", "b2g", "b2gv",
            "b2sc", "b2scv", "b2d", "b2L", "b2eb", "b2ea",
            "b2G", "b2ce", "b2y", "b2ye", "b2ys", "b2r0", "b2rs",
            "dts", "cl", "cs1", "cv1", "cm", "cbc", "cn",
            "dw", "dl", "dmm", "dev", "dtr", "dte", "dst",
            "dmp",
        ]
        S = {n: E_(nc.semaphore(n)) for n in sem_names}

        blk = E_(nc.Block())

        def p0(role, eng, PS="p0", x_src=None):
            x_src = xT if x_src is None else x_src
            if role == "sync":
                consts = [] if PS != "p0" else [(k_id, c_ident), (k_sl, c_sl),
                                  (k_negid, c_negid), (k_ones, c_ones),
                                  (i4sb[:], c_id4), (selg[:], c_selg),
                                  (idf, c_identf), (cwsb[:], conv_w4),
                                  (cbsb[:], conv_b), (vecs[:, 0:1], dtb_i),
                                  (vecs[:, 1:2], Alog_i), (vecs[:, 3:4], cmask_i),
                                  (vecs[:, 4:5], umask_i), (dxsb[:], Dexp_i),
                                  (nwsb[:], nwcol_i), (bcsb[:], bias_i)]
                for dst, srcv in consts:
                    eng.dma_start(out=dst, in_=srcv).then_inc(S["cst"], 16)
                for kb in range(32):
                    if kb >= 1:
                        eng.wait_ge(S[PS + "b"], 2 * kb)
                    eng.dma_start(out=xf_st, in_=x_src[kb * 128:(kb + 1) * 128, :]
                                  ).then_inc(S[PS + "x"], 16)
                if PS == "p0":
                    eng.wait_ge(S["cst"], 16 * 17)
            elif role == "gpsimd":
                if PS == "p0":
                    eng.dma_start(out=gcsb[:], in_=gate_col).then_inc(S["cst"], 16)
                    eng.memset(zrf[:], 0.0)
                    eng.memset(kbias[:, 0:1], 1e-5)
                    eng.memset(kbias[:, 1:2], 1.0).then_inc(S["kc"], 1)
            elif role == "scalar":
                for kb in range(32):
                    eng.wait_ge(S[PS + "x"], 16 * (kb + 1))
                    if kb >= 2:
                        eng.wait_ge(S[PS + "m"], kb - 1)
                    eng.activation(xn[:, kb, :], xf_st, AF.Copy).then_inc(S[PS + "b"], 1)
                    eng.activation(sq_st, xf_st, AF.Square).then_inc(S[PS + "b"], 1)
                eng.wait_ge(S[PS + "s"], 1)
                if PS == "p0":
                    eng.wait_ge(S["kc"], 1)
                eng.activation(stat[0:1, 2 * TH:3 * TH], stat[0:1, TH:2 * TH],
                               AF.Ln, bias=kbias[0:1, 0:1])
                eng.activation(stat[0:1, 2 * TH:3 * TH], stat[0:1, 2 * TH:3 * TH],
                               AF.Exp, scale=-0.5)
                eng.activation(statb[0:1, 0:TH], stat[0:1, 0:TH], AF.Copy)
                eng.activation(statb[0:1, TH:2 * TH], stat[0:1, 2 * TH:3 * TH],
                               AF.Copy).then_inc(S[PS + "s"], 1)
            elif role == "tensor":
                for kb in range(32):
                    eng.wait_ge(S[PS + "b"], 2 * (kb + 1))
                    for j, (j0, j1) in enumerate(JC):
                        w = j1 - j0
                        eng.matmul(pb[j][0:1, 0:w], k_ones[:, 0:1], xn[:, kb, j0:j1],
                                   start=(kb == 0), stop=(kb == 31))
                        mm = eng.matmul(pb[3 + j][0:1, 0:w], k_ones[:, 0:1],
                                        sq_st[:, j0:j1], start=(kb == 0),
                                        stop=(kb == 31))
                    mm.then_inc(S[PS + "m"], 1)
                eng.wait_ge(S[PS + "s"], 2)
                for j, (j0, j1) in enumerate(JC):
                    w = j1 - j0
                    eng.matmul(pb[j][:, 0:w], k_ones[0:1, :], statb[0:1, j0:j1],
                               start=True, stop=True)
                    mm = eng.matmul(pb[3 + j][:, 0:w], k_ones[0:1, :],
                                    statb[0:1, TH + j0:TH + j1], start=True, stop=True)
                mm.then_inc(S[PS + "c"], 1)
            elif role == "vector":
                eng.wait_ge(S[PS + "m"], 32)
                for j, (j0, j1) in enumerate(JC):
                    w = j1 - j0
                    eng.tensor_scalar(stat[0:1, j0:j1], pb[j][0:1, 0:w],
                                      1.0 / D, None, ALU.mult)
                    eng.tensor_scalar(stat[0:1, TH + j0:TH + j1], pb[3 + j][0:1, 0:w],
                                      1.0 / D, None, ALU.mult)
                eng.tensor_mul(stat[0:1, 2 * TH:3 * TH], stat[0:1, 0:TH],
                               stat[0:1, 0:TH])
                eng.tensor_sub(stat[0:1, TH:2 * TH], stat[0:1, TH:2 * TH],
                               stat[0:1, 2 * TH:3 * TH]).then_inc(S[PS + "s"], 1)
                eng.wait_ge(S[PS + "c"], 1)
                for kb in range(32):
                    for j, (j0, j1) in enumerate(JC):
                        w = j1 - j0
                        eng.tensor_sub(xn[:, kb, j0:j1], xn[:, kb, j0:j1],
                                       pb[j][:, 0:w])
                        eng.tensor_mul(xn[:, kb, j0:j1], xn[:, kb, j0:j1],
                                       pb[3 + j][:, 0:w])

        def a_ph(role, eng, PS="a", ets=ETS, scr=hbc_scr, full=True,
                 gam=gamr, dd=b_ddec):
            nh = sum(1 for _, kk in ets if kk == "hbc")
            if role == "gpsimd":
                for i, (e0, knd) in enumerate(ets):
                    if i >= 3:
                        eng.wait_ge(S[PS + "mm"], i - 2)
                    eng.dma_start(
                        out=wring[i % 3].rearrange("p (kb e) -> p kb e", kb=32),
                        in_=WinT[:, e0:e0 + 128].rearrange("(kb p) e -> p kb e", p=128)
                        ).then_inc(S[PS + "w"], 16)
            elif role == "tensor":
                for i, (e0, knd) in enumerate(ets):
                    eng.wait_ge(S[PS + "w"], 16 * (i + 1))
                    if i >= 2:
                        eng.wait_ge(S[PS + "pc"], i - 1)
                    s3 = 3 * (i % 2)
                    wr = wring[i % 3].rearrange("p (kb e) -> p kb e", kb=32)
                    for k in range(32):
                        for j, (j0, j1) in enumerate(JC):
                            mm = eng.matmul(pb[s3 + j][:, 0:j1 - j0], wr[:, k, :],
                                            xn[:, k, j0:j1], start=(k == 0),
                                            stop=(k == 31))
                    mm.then_inc(S[PS + "mm"], 1)
                eng.wait_ge(S[PS + "dt"], 5)
                eng.transpose(pbt[0:8, 0:128], gbf[:], k_id).then_inc(S[PS + "gt"], 1)
            elif role == "vector":
                eng.wait_ge(S[PS + "mm"], 1)
                for j, (j0, j1) in enumerate(JC):
                    ins = eng.tensor_copy(f_dtr[:, j0:j1], pb[j][:, 0:j1 - j0])
                ins.then_inc(S[PS + "pc"], 1)
                eng.tensor_scalar(f_e[:, 0:1024], f_dtr[:, 3:1027], vecs[:, 0:1],
                                  None, ALU.add)
                eng.tensor_scalar(f_m[:, 0:1024], f_e[:, 0:1024], 0.0, None,
                                  ALU.max).then_inc(S[PS + "dtv"], 1)
                eng.wait_ge(S[PS + "dt"], 3)
                eng.tensor_add(f_dt[:, 0:1024], f_m[:, 0:1024], f_dtr[:, 0:1024])
                eng.tensor_scalar(f_dt[:, 0:1024], f_dt[:, 0:1024], 100.0, None,
                                  ALU.min).then_inc(S[PS + "dtv"], 1)
                eng.wait_ge(S[PS + "dt"], 4)
                eng.tensor_scalar(f_m[:, 0:1024], f_dt[:, 0:1024], vecs[:, 5:6],
                                  None, ALU.mult)
                for z in range(NZ):
                    eng.tensor_tensor_scan(f_ac[:, z * 128:(z + 1) * 128],
                                           f_m[:, z * 128:(z + 1) * 128], zrf[:],
                                           0.0, ALU.add, ALU.add)
                for z in range(NZ):
                    ins = eng.tensor_copy(csg[:, z:z + 1],
                                          f_ac[:, z * 128 + 127:z * 128 + 128])
                ins.then_inc(S[PS + "dtv"], 1)
                eng.wait_ge(S[PS + "dtv"], 3)
                for z in range(NZ):
                    ins = eng.tensor_scalar(f_e[:, z * 128:(z + 1) * 128],
                                            f_ac[:, z * 128:(z + 1) * 128],
                                            csg[:, z:z + 1], None, ALU.subtract)
                ins.then_inc(S[PS + "dtv"], 1)
                eng.wait_ge(S[PS + "dt"], 6)
                eng.tensor_mul(dd[:], f_dt[:, 0:1024], f_m[:, 0:1024])
                if full:
                    eng.wait_ge(S[PS + "dt"], 7)
                    eng.tensor_scalar(f_e[:, 0:1024], f_e[:, 0:1024], NEG_BIG,
                                      None, ALU.max)
                    eng.tensor_sub(f_dt[:, 0:1024], f_e[:, 0:1024],
                                   f_ac[:, 0:1024])
                    eng.tensor_copy(ard0[:], f_dt[:, 0:1024]
                                    ).then_inc(S[PS + "dtv"], 1)
                    eng.tensor_sub(f_m[:, 0:1024], f_dt[:, 0:1024], ard0[:])
                    eng.tensor_copy(ard1[:], f_m[:, 0:1024]
                                    ).then_inc(S[PS + "dtv"], 1)
                    eng.tensor_sub(f_dt[:, 0:1024], f_m[:, 0:1024], ard1[:])
                    eng.wait_ge(S["dts"], 16)
                    eng.tensor_copy(ard0[:], f_dt[:, 0:1024]
                                    ).then_inc(S[PS + "dtv"], 1)
                    eng.wait_ge(S["dts"], 32)
                    eng.tensor_copy(ard1[:], f_ac[:, 0:1024]
                                    ).then_inc(S[PS + "dtv"], 1)
                    eng.tensor_sub(f_m[:, 0:1024], f_ac[:, 0:1024], ard1[:])
                    eng.wait_ge(S["dts"], 48)
                    eng.tensor_copy(ard0[:], f_m[:, 0:1024]
                                    ).then_inc(S[PS + "dtv"], 1)
                    eng.tensor_sub(f_dt[:, 0:1024], f_m[:, 0:1024], ard0[:])
                    eng.wait_ge(S["dts"], 64)
                    eng.tensor_copy(ard1[:], f_dt[:, 0:1024]
                                    ).then_inc(S[PS + "dtv"], 1)
                for a in range(nh):
                    i = 1 + a
                    eng.wait_ge(S[PS + "pc"], i + 1)
                    if a >= 2:
                        eng.wait_ge(S[PS + "slu"], a - 1)
                    cv = cvst[a % 2]
                    co = cvo[a % 2]
                    eng.tensor_scalar(co[:], cv[:, 0:1024],
                                      cwsb[:, a * 4:a * 4 + 1], None, ALU.mult)
                    for kk in range(1, 4):
                        ins = eng.scalar_tensor_tensor(
                            co[:], cv[:, kk:kk + 1024],
                            cwsb[:, a * 4 + kk:a * 4 + kk + 1],
                            co[:], ALU.mult, ALU.add)
                    ins.then_inc(S[PS + "cnv"], 1)
            elif role == "scalar":
                eng.wait_ge(S[PS + "dtv"], 1)
                eng.activation(f_dtr[:, 0:1024], f_e[:, 0:1024], AF.Abs)
                eng.activation(f_e[:, 0:1024], f_dtr[:, 0:1024], AF.Exp, scale=-1.0)
                eng.activation(f_dtr[:, 0:1024], f_e[:, 0:1024], AF.Ln,
                               bias=kbias[:, 1:2]).then_inc(S[PS + "dt"], 3)
                eng.activation(vecs[:, 2:3], vecs[:, 1:2], AF.Exp)
                eng.activation(vecs[:, 5:6], vecs[:, 2:3], AF.Copy, scale=-1.0
                               ).then_inc(S[PS + "dt"], 1)
                eng.wait_ge(S[PS + "dtv"], 3)
                eng.activation(gbf[:], csg[:], AF.Exp).then_inc(S[PS + "dt"], 1)
                eng.wait_ge(S[PS + "dtv"], 4)
                eng.activation(f_m[:, 0:1024], f_e[:, 0:1024], AF.Exp, scale=-1.0
                               ).then_inc(S[PS + "dt"], 1)
                eng.activation(f_e[:, 0:1024], f_dt[:, 0:1024], AF.Ln
                               ).then_inc(S[PS + "dt"], 1)

                for i, (e0, knd) in enumerate(ets):
                    if knd == "dt":
                        continue
                    s3 = 3 * (i % 2)
                    eng.wait_ge(S[PS + "mm"], i + 1)
                    if knd == "hbc":
                        a = i - 1
                        if a >= 2:
                            eng.wait_ge(S[PS + "st"], 16 * (a - 1))
                        cv = cvst[a % 2]
                        for j, (j0, j1) in enumerate(JC):
                            ins = eng.activation(cv[:, j0:j1],
                                                 pb[s3 + j][:, 0:j1 - j0], AF.Copy)
                        ins.then_inc(S[PS + "pc"], 1)
                        eng.wait_ge(S[PS + "cnv"], a + 1)
                        eng.activation(cv[:, 0:1024], cvo[a % 2][:], AF.Silu,
                                       bias=cbsb[:, a:a + 1]).then_inc(S[PS + "slu"], 1)
                    else:
                        gi = i - 1 - nh
                        if gi >= 2:
                            eng.wait_ge(S[PS + "st"], 16 * (nh + gi - 1))
                        gs = gst[gi % 2]
                        eng.activation(gs[:, 0:340], pb[s3 + 0][:, 3:343], AF.Copy)
                        eng.activation(gs[:, 340:682], pb[s3 + 1][:, 0:342], AF.Copy)
                        eng.activation(gs[:, 682:1024], pb[s3 + 2][:, 0:342], AF.Copy
                                       ).then_inc(S[PS + "pc"], 1)
                eng.wait_ge(S[PS + "gt"], 1)
                eng.activation(gts[:], pbt[0:8, 0:128], AF.Copy).then_inc(S[PS + "ge"], 1)
            elif role == "sync":
                if full:
                    for pi, (sem_v, ab) in enumerate(
                            [(5, 0), (6, 1), (7, 0), (8, 1), (9, 0), (10, 1)]):
                        eng.wait_ge(S[PS + "dtv"], sem_v)
                        eng.dma_start(out=dt_scr[pi * 128:(pi + 1) * 128, :],
                                      in_=(ard0 if ab == 0 else ard1)
                                      ).then_inc(S["dts"], 16)

                for a in range(nh):
                    eng.wait_ge(S[PS + "slu"], a + 1)
                    eng.dma_start(out=scr[a * 128:(a + 1) * 128, :],
                                  in_=cvst[a % 2][:, 0:1024]).then_inc(S[PS + "st"], 16)
                if full:
                    for gi in range(64):
                        eng.wait_ge(S[PS + "pc"], 2 + nh + gi)
                        eng.dma_start(out=gate_scr[gi * 128:(gi + 1) * 128, :],
                                      in_=gst[gi % 2][:]).then_inc(S[PS + "st"], 16)
                eng.wait_ge(S[PS + "ge"], 1)
                eng.dma_start(out=gam.rearrange("o (z h) -> o z h", z=8),
                              in_=gts[:]).then_inc(S[PS + "gr"], 16)

        def b_transposes(eng, P, z, dd=None):
            dd = b_ddec if dd is None else dd
            eng.wait_ge(S[P + "e"], 2 * z)
            eng.transpose(pbt[:, 0:128], dd[:, z * 128:(z + 1) * 128], k_id
                          ).then_inc(S[P + "t"], 1)
            eng.wait_ge(S[P + "e"], 2 * z + 1)
            for g in range(G):
                mm = eng.transpose(pbt[:, g * 128:(g + 1) * 128],
                                   bfc[:, (z % 2) * 2048 + g * 128:
                                       (z % 2) * 2048 + (g + 1) * 128], k_id)
            mm.then_inc(S[P + "t"], 1)

        def b_tr_evac(eng, P, z):
            eng.wait_ge(S[P + "t"], 2 * z + 1)
            if z >= 1:
                eng.wait_ge(S[P + "x"], z)
            eng.activation(ddsb[:], pbt[:, 0:128], AF.Copy).then_inc(S[P + "e"], 1)
            eng.wait_ge(S[P + "t"], 2 * z + 2)
            if z >= 1:
                eng.wait_ge(S[P + "s"], 8 * z)
            eng.activation(btsb[:], pbt[:], AF.Copy).then_inc(S[P + "e"], 1)

        def b_xchd(eng, P, z, hT, nload):
            eng.wait_ge(S[P + "l"], nload * (z + 1))
            eng.wait_ge(S[P + "e"], 2 * z + 1)
            if z >= 1:
                eng.wait_ge(S[P + "s"], 8 * z)
            eng.tensor_tensor(
                xchd.rearrange("p (h pp) -> p h pp", h=128),
                hT[z % 2].rearrange("p (h pp) -> p h pp", h=128),
                ddsb[:, :, None].broadcast_to([128, 128, PH]),
                ALU.mult).then_inc(S[P + "x"], 1)

        def b_states_mm(eng, P, z, g, off):
            for hh in range(16):
                h = g * 16 + hh
                bank = pb[off + hh // 8]
                mm = eng.matmul(bank[:, (hh % 8) * 64:(hh % 8 + 1) * 64],
                                btsb[:, g * 128:(g + 1) * 128],
                                xchd[:, h * 64:(h + 1) * 64], start=True, stop=True)
            mm.then_inc(S[P + "s"], 1)

        def b_gbc_mm(eng, P, z, gam=None):
            gam = gamr if gam is None else gam
            if z > 0:
                eng.wait_ge(S[P + "gv"], z)
            eng.matmul(pbs[:, 128:256], k_ones[0:1, :],
                       gam[0:1, z * 128:(z + 1) * 128],
                       start=True, stop=True).then_inc(S[P + "g"], 1)

        def b_gbc_evac(eng, P, z):
            eng.wait_ge(S[P + "g"], z + 1)
            if z > 0:
                eng.wait_ge(S[P + "r"], 8 * z)
            eng.activation(gbc_sb[:], pbs[:, 128:256], AF.Copy
                           ).then_inc(S[P + "gv"], 1)

        def b_rupd_g(eng, P, z, g, off, with_y):
            zg = 8 * z + g
            eng.wait_ge(S[P + "gv"], z + 1)
            if with_y:
                eng.wait_ge(S["b2y"], zg + 1)
            eng.wait_ge(S[P + "s"], zg + 1)
            rv = r_f[:, g * 1024:(g + 1) * 1024].rearrange(
                "p (h pp) -> p h pp", h=16)
            eng.tensor_tensor(rv, rv,
                              gbc_sb[:, g * 16:(g + 1) * 16, None]
                              .broadcast_to([128, 16, PH]), ALU.mult)
            eng.tensor_add(r_f[:, g * 1024:g * 1024 + 512],
                           r_f[:, g * 1024:g * 1024 + 512], pb[off][:])
            eng.tensor_add(r_f[:, g * 1024 + 512:(g + 1) * 1024],
                           r_f[:, g * 1024 + 512:(g + 1) * 1024],
                           pb[off + 1][:]).then_inc(S[P + "r"], 1)

        def b1(role, eng):
            if role == "sync":
                eng.wait_ge(S["qgr"], 16)
                eng.wait_ge(S["qst"], 16 * 72)
                for z in range(NZ):
                    if z >= 1:
                        eng.wait_ge(S["b1x"], z)
                    eng.dma_start_transpose(b1hsT[z % 2],
                                            hbc2_scr[0:INTER, z * 128:(z + 1) * 128]
                                            ).then_inc(S["b1l"], 16)
                    if z >= 2:
                        eng.wait_ge(S["b1s"], 8 * (z - 1))
                    eng.dma_start(out=bfc[:, (z % 2) * 2048:(z % 2) * 2048 + 1024]
                                  .rearrange("p (g s) -> p g s", g=8),
                                  in_=hbc2_scr[INTER:INTER + 1024,
                                               z * 128:(z + 1) * 128]
                                  .rearrange("(g n) s -> n g s", n=128)
                                  ).then_inc(S["b1l"], 16)
            elif role == "tensor":
                for z in range(NZ):
                    eng.wait_ge(S["b1l"], 32 * (z + 1))
                    b_transposes(eng, "b1", z, dd=ddec2)
                    b_gbc_mm(eng, "b1", z, gam=gamr2)
                    eng.wait_ge(S["b1x"], z + 1)
                    eng.wait_ge(S["b1e"], 2 * (z + 1))
                    for g in range(G):
                        if g >= 2:
                            eng.wait_ge(S["b1r"], 8 * z + g - 1)
                        b_states_mm(eng, "b1", z, g, 2 * (g % 2))
            elif role == "scalar":
                for z in range(NZ):
                    b_tr_evac(eng, "b1", z)
                    b_gbc_evac(eng, "b1", z)
            elif role == "vector":
                eng.memset(r_f, 0.0)
                for z in range(NZ):
                    b_xchd(eng, "b1", z, b1hsT, 32)
                    for g in range(G):
                        b_rupd_g(eng, "b1", z, g, 2 * (g % 2), False)
                eng.tensor_scalar(r_f, r_f, vecs[:, 4:5], None, ALU.mult)

        def e_ph(role, eng):
            if role == "gpsimd":
                eng.dma_start(out=asT.rearrange("p (kb r) -> p kb r", kb=32),
                              in_=loraAst.rearrange("(kb p) r -> p kb r", p=128)
                              ).then_inc(S["elp"], 16)
                eng.dma_start(out=arb[32:64, 0:4096], in_=loraBst
                              ).then_inc(S["elp"], 16)
                for grp in range(4):
                    if grp == 3:
                        eng.wait_ge(S["elt"], 2)
                    eng.dma_start(
                        out=wring[grp % 3].rearrange("p (kb r) -> p kb r", kb=32),
                        in_=loraAt4[grp].rearrange("(kb p) r -> p kb r", p=128)
                        ).then_inc(S["elp"], 16)
                for idx in range(8):
                    eng.wait_ge(S["elt"], 5)
                    if idx >= 6:
                        eng.wait_ge(S["elb"], idx - 5)
                    oh, grp = idx // 4, idx % 4
                    eng.dma_start(out=w6[idx % 6][:, 0:2048],
                                  in_=loraBperm[grp * 128:(grp + 1) * 128,
                                                oh * 2048:(oh + 1) * 2048]
                                  ).then_inc(S["elp"], 16)
                for ot in range(32):
                    if ot < 3:
                        eng.wait_ge(S["elb"], 8)
                    else:
                        eng.wait_ge(S["emm"], ot - 2)
                    eng.dma_start(
                        out=wring[ot % 3].rearrange("p (kb e) -> p kb e", kb=32),
                        in_=kern[:, ot * 128:(ot + 1) * 128]
                        .rearrange("(kb p) e -> p kb e", p=128)
                        ).then_inc(S["ew"], 16)
            elif role == "tensor":
                for grp in range(4):
                    mm = eng.matmul(pbs[:, 384 + grp:385 + grp],
                                    selg[:, grp * 128:(grp + 1) * 128], gcsb[:],
                                    start=True, stop=True)
                mm.then_inc(S["elt"], 1)
                for grp in range(4):
                    eng.wait_ge(S["elp"], 16 * (3 + grp))
                    if grp >= 2:
                        eng.wait_ge(S["ele"], grp - 1)
                    wr = wring[grp % 3].rearrange("p (kb r) -> p kb r", kb=32)
                    pA, pB = pb[2 * (grp % 2)], pb[2 * (grp % 2) + 1]
                    for k in range(32):
                        eng.matmul(pA[:], wr[:, k, :], xn[:, k, 3:515],
                                   start=(k == 0), stop=(k == 31))
                        mm = eng.matmul(pB[:], wr[:, k, :], xn[:, k, 515:1027],
                                        start=(k == 0), stop=(k == 31))
                    mm.then_inc(S["elt"], 1)
                eng.wait_ge(S["elv"], 2)
                eng.wait_ge(S["ele"], 4)
                for grp in range(4):
                    eng.matmul(pb[4][0:32, :], gdsb[:, grp * 32:(grp + 1) * 32],
                               e_axall[:, grp, 0:512], start=(grp == 0),
                               stop=(grp == 3))
                    eng.matmul(pb[5][0:32, :], gdsb[:, grp * 32:(grp + 1) * 32],
                               e_axall[:, grp, 512:1024], start=(grp == 0),
                               stop=(grp == 3))
                eng.wait_ge(S["elp"], 16)
                aT = asT.rearrange("p (kb r) -> p kb r", kb=32)
                for k in range(32):
                    eng.matmul(pb[4][32:64, :], aT[:, k, :], xn[:, k, 3:515],
                               start=(k == 0), stop=(k == 31))
                    mm = eng.matmul(pb[5][32:64, :], aT[:, k, :], xn[:, k, 515:1027],
                                    start=(k == 0), stop=(k == 31))
                mm.then_inc(S["elt"], 1)
                for idx in range(8):
                    eng.wait_ge(S["elp"], 16 * (6 + idx + 1))
                    oh, grp = idx // 4, idx % 4
                    if idx == 4:
                        eng.wait_ge(S["ele"], 6)
                    for jj in range(4):
                        mm = eng.matmul(pb[jj][0:32, :],
                                        gdsb[:, grp * 32:(grp + 1) * 32],
                                        w6[idx % 6][:, jj * 512:(jj + 1) * 512],
                                        start=(grp == 0), stop=(grp == 3))
                    mm.then_inc(S["elb"], 1)
                eng.wait_ge(S["ele"], 7)
                for ot in range(32):
                    eng.wait_ge(S["ew"], 16 * (ot + 1))
                    if ot >= 2:
                        eng.wait_ge(S["eev"], ot - 1)
                    s2 = 2 * (ot % 2)
                    wr = wring[ot % 3].rearrange("p (kb e) -> p kb e", kb=32)
                    for k in range(32):
                        eng.matmul(pb[s2][:], wr[:, k, :], xn[:, k, 3:515],
                                   start=(k == 0), stop=False)
                        eng.matmul(pb[s2 + 1][:], wr[:, k, :], xn[:, k, 515:1027],
                                   start=(k == 0), stop=False)
                    eng.matmul(pb[s2][:], e_bcat[0:64, ot * 128:(ot + 1) * 128],
                               e_axc[0:64, 0:512], start=False, stop=True)
                    eng.matmul(pb[s2 + 1][:], e_bcat[0:64, ot * 128:(ot + 1) * 128],
                               e_axc[0:64, 512:1024], start=False, stop=True
                               ).then_inc(S["emm"], 1)
            elif role == "vector":
                eng.wait_ge(S["elt"], 1)
                eng.tensor_copy(gcolf[:], pbs[:, 384:388]).then_inc(S["elv"], 1)
                eng.wait_ge(S["elv"], 1)
                for grp in range(4):
                    ins = eng.tensor_scalar(gdsb[:, grp * 32:(grp + 1) * 32],
                                            i4sb[:], gcolf[:, grp:grp + 1], None,
                                            ALU.mult)
                ins.then_inc(S["elv"], 1)
                for ot in range(32):
                    eng.wait_ge(S["emm"], ot + 1)
                    if ot >= 2:
                        eng.wait_ge(S["est"], 16 * (ot - 1))
                    s2 = 2 * (ot % 2)
                    eng.tensor_scalar(e_st[ot % 2][:, 0:512], pb[s2][:],
                                      bcsb[:, ot:ot + 1], None, ALU.add)
                    eng.tensor_scalar(e_st[ot % 2][:, 512:1024], pb[s2 + 1][:],
                                      bcsb[:, ot:ot + 1], None, ALU.add
                                      ).then_inc(S["eev"], 1)
            elif role == "scalar":
                for grp in range(4):
                    eng.wait_ge(S["elt"], 2 + grp)
                    eng.activation(e_axall[:, grp, 0:512], pb[2 * (grp % 2)][:],
                                   AF.Copy)
                    eng.activation(e_axall[:, grp, 512:1024],
                                   pb[2 * (grp % 2) + 1][:], AF.Copy
                                   ).then_inc(S["ele"], 1)
                eng.wait_ge(S["elt"], 6)
                eng.activation(e_axc[0:64, 0:512], pb[4][0:64, :], AF.Copy)
                eng.activation(e_axc[0:64, 512:1024], pb[5][0:64, :], AF.Copy
                               ).then_inc(S["ele"], 1)
                for oh in range(2):
                    eng.wait_ge(S["elb"], 4 * (oh + 1))
                    for jj in range(4):
                        ins = eng.activation(
                            e_bcat[0:32, oh * 2048 + jj * 512:
                                   oh * 2048 + (jj + 1) * 512],
                            pb[jj][0:32, :], AF.Copy)
                    ins.then_inc(S["ele"], 1)
            elif role == "sync":
                for ot in range(32):
                    eng.wait_ge(S["eev"], ot + 1)
                    eng.dma_start(out=ffn_scr[ot * 128:(ot + 1) * 128, :],
                                  in_=e_st[ot % 2][:]).then_inc(S["est"], 16)

        def b2(role, eng):
            zs = lambda z: slice(z * 128, (z + 1) * 128)
            if role == "sync":
                eng.dma_start(out=bi16, in_=c_bi16).then_inc(S["agr"], 16)
                eng.wait_ge(S["dts"], 96)
                for z in range(NZ):
                    if z >= 2:
                        eng.wait_ge(S["b2y"], 8 * (z - 1))
                        eng.wait_ge(S["b2x"], z - 1)
                    eng.dma_start_transpose(hsT[z % 2], hbc_scr[0:INTER, zs(z)]
                                            ).then_inc(S["b2l"], 16)
                    if z >= 2:
                        eng.wait_ge(S["b2s"], 8 * (z - 1))
                        eng.wait_ge(S["b2ce"], 8 * (z - 1))
                    eng.dma_start(out=bfc[:, (z % 2) * 2048:(z % 2) * 2048 + 2048]
                                  .rearrange("p (g s) -> p g s", g=16),
                                  in_=hbc_scr[INTER:CONVD, zs(z)]
                                  .rearrange("(g n) s -> n g s", n=128)
                                  ).then_inc(S["b2l"], 16)
                    if z >= 1:
                        eng.wait_ge(S["b2d"], 8 * z)
                    for q in range(3):
                        eng.dma_start(out=colvst[q * 16:(q + 1) * 16, :]
                                      .rearrange("h (g s) -> h g s", g=8),
                                      in_=dt_scr[q * 128:(q + 1) * 128, zs(z)]
                                      .rearrange("(g h) l -> h g l", h=16)
                                      ).then_inc(S["b2l"], 16)
                    for g in range(G):
                        zg = 8 * z + g
                        if zg >= 1:
                            eng.wait_ge(S["b2d"], zg)
                            eng.wait_ge(S["b2eb"], zg)
                        eng.dma_start(out=rowsa.rearrange("o (h l) -> o h l", h=16),
                                      in_=dt_scr[384:768, zs(z)]
                                      .rearrange("(q gh) l -> q gh l", gh=128)
                                      [:, g * 16:(g + 1) * 16, :]
                                      ).then_inc(S["b2rs"], 16)
                    eng.wait_ge(S["b2ye"], 16 * (z + 1))
                    eng.dma_start(out=y_scr.rearrange("(j p) t -> p j t", p=128)
                                  [:, :, zs(z)],
                                  in_=yst.rearrange("p (j l) -> p j l", j=64)
                                  ).then_inc(S["b2ys"], 16)
            elif role == "tensor":
                eng.wait_ge(S["agr"], 32)
                for z in range(NZ):
                    eng.wait_ge(S["b2l"], 80 * (z + 1))
                    b_transposes(eng, "b2", z)
                    b_gbc_mm(eng, "b2", z)
                    eng.wait_ge(S["b2e"], 2 * (z + 1))
                    eng.wait_ge(S["b2gv"], z + 1)
                    for g in range(G):
                        zg = 8 * z + g
                        if zg >= 1:
                            eng.wait_ge(S["b2scv"], zg)
                        eng.matmul(pbs[:, 0:128],
                                   bfc[:, (z % 2) * 2048 + g * 128:
                                       (z % 2) * 2048 + (g + 1) * 128],
                                   bfc[:, (z % 2) * 2048 + 1024 + g * 128:
                                       (z % 2) * 2048 + 1024 + (g + 1) * 128],
                                   start=True, stop=True).then_inc(S["b2sc"], 1)
                        if zg >= 1:
                            eng.wait_ge(S["b2ea"], zg)
                            eng.wait_ge(S["b2r"], 8 * z + g)
                        eng.wait_ge(S["b2rs"], 16 * (zg + 1))
                        for jj in range(4):
                            eng.matmul(pb[jj][:], colvst[:, g * 128:(g + 1) * 128],
                                       bi16[:, jj * 512:(jj + 1) * 512],
                                       start=True, stop=False)
                            eng.matmul(pb[jj][:], k_ones[0:3, 0:128],
                                       rowsa[0:3, jj * 512:(jj + 1) * 512],
                                       start=False, stop=False)
                            mm = eng.matmul(pb[jj][:].rearrange("p (a l) -> p a l",
                                                                a=4),
                                            k_sl,
                                            k_negid[:, None, :]
                                            .broadcast_to([128, 4, 128]),
                                            start=False, stop=True)
                        mm.then_inc(S["b2d"], 1)
                        eng.wait_ge(S["b2L"], zg + 1)
                        for jj in range(4):
                            mm = eng.matmul(pb[jj][:], k_ones[0:3, 0:128],
                                            rowsa[0:3, jj * 512:(jj + 1) * 512],
                                            start=True, stop=True)
                        mm.then_inc(S["b2eb"], 1)
                        eng.wait_ge(S["b2G"], zg + 1)
                        eng.wait_ge(S["b2ce"], zg + 1)
                        eng.wait_ge(S["b2ye"], 16 * z + 2 * g)
                        if z >= 1:
                            eng.wait_ge(S["b2r"], 8 * (z - 1) + g + 1)
                        for hh in range(16):
                            h = g * 16 + hh
                            bank = pb[4 + hh // 8]
                            base = (hh % 2) * 64
                            slot = (hh // 2) % 4
                            eng.matmul(bank[base:base + 64,
                                            slot * 128:(slot + 1) * 128],
                                       hsT[z % 2][:, h * 64:(h + 1) * 64],
                                       l_sb[:, hh * 128:(hh + 1) * 128],
                                       start=True, stop=False)
                            mm = eng.matmul(bank[base:base + 64,
                                                 slot * 128:(slot + 1) * 128],
                                            r_f[:, h * 64:(h + 1) * 64],
                                            ce_sb[:, hh * 128:(hh + 1) * 128],
                                            start=False, stop=True)
                        mm.then_inc(S["b2y"], 1)
                        eng.wait_ge(S["b2ea"], zg + 1)
                        eng.wait_ge(S["b2x"], z + 1)
                        b_states_mm(eng, "b2", z, g, 0)
            elif role == "scalar":
                for z in range(NZ):
                    b_tr_evac(eng, "b2", z)
                    b_gbc_evac(eng, "b2", z)
                    for g in range(G):
                        zg = 8 * z + g
                        eng.wait_ge(S["b2sc"], zg + 1)
                        if zg >= 1:
                            eng.wait_ge(S["b2G"], zg)
                        eng.activation(ssb[:], pbs[:, 0:128], AF.Copy
                                       ).then_inc(S["b2scv"], 1)
                        eng.wait_ge(S["b2d"], zg + 1)
                        if zg >= 1:
                            eng.wait_ge(S["b2y"], zg)
                        for jj in range(4):
                            ins = eng.activation(l_sb[:, jj * 512:(jj + 1) * 512],
                                                 pb[jj][:], AF.Exp)
                        ins.then_inc(S["b2L"], 1)
                        eng.wait_ge(S["b2eb"], zg + 1)
                        if zg >= 1:
                            eng.wait_ge(S["b2y"], zg)
                        for jj in range(4):
                            ins = eng.activation(ce_sb[:, jj * 512:(jj + 1) * 512],
                                                 pb[jj][:], AF.Exp)
                        ins.then_inc(S["b2ea"], 1)
                        eng.wait_ge(S["b2y"], zg + 1)
                        if z >= 1:
                            eng.wait_ge(S["b2ys"], 16 * z)
                        eng.activation(yst[:, (g * 8) * 128:(g * 8 + 4) * 128],
                                       pb[4][:], AF.Copy)
                        eng.activation(yst[:, (g * 8 + 4) * 128:(g * 8 + 8) * 128],
                                       pb[5][:], AF.Copy).then_inc(S["b2ye"], 2)
            elif role == "vector":
                for z in range(NZ):
                    b_xchd(eng, "b2", z, hsT, 80)
                    for g in range(G):
                        zg = 8 * z + g
                        eng.wait_ge(S["b2L"], zg + 1)
                        eng.wait_ge(S["b2scv"], zg + 1)
                        eng.tensor_tensor(
                            l_sb.rearrange("p (hh l) -> p hh l", hh=16),
                            l_sb.rearrange("p (hh l) -> p hh l", hh=16),
                            ssb[:, None, :].broadcast_to([128, 16, 128]),
                            ALU.mult).then_inc(S["b2G"], 1)
                        eng.wait_ge(S["b2ea"], zg + 1)
                        eng.tensor_tensor(
                            ce_sb.rearrange("p (hh l) -> p hh l", hh=16),
                            ce_sb.rearrange("p (hh l) -> p hh l", hh=16),
                            bfc[:, None, (z % 2) * 2048 + 1024 + g * 128:
                                (z % 2) * 2048 + 1024 + (g + 1) * 128]
                            .broadcast_to([128, 16, 128]),
                            ALU.mult).then_inc(S["b2ce"], 1)
                        b_rupd_g(eng, "b2", z, g, 0, True)

        def c_ph(role, eng):
            if role == "sync":
                eng.wait_ge(S["b2ys"], 16 * NZ)
                for it in range(16):
                    fg, th = it // 2, it % 2
                    if it >= 1:
                        eng.wait_ge(S["cn"], 8 * it)
                    tsl = slice(th * 512, (th + 1) * 512)
                    eng.dma_start(out=c_y.rearrange("p (a t) -> p a t", a=8),
                                  in_=y_scr[fg * 1024:(fg + 1) * 1024, tsl]
                                  .rearrange("(a p) t -> p a t", p=128)
                                  ).then_inc(S["cl"], 16)
                    eng.dma_start(out=c_g.rearrange("p (a t) -> p a t", a=8),
                                  in_=gate_scr[fg * 1024:(fg + 1) * 1024, tsl]
                                  .rearrange("(a p) t -> p a t", p=128)
                                  ).then_inc(S["cl"], 16)
                    eng.dma_start(out=c_h.rearrange("p (a t) -> p a t", a=8),
                                  in_=hbc_scr[fg * 1024:(fg + 1) * 1024, tsl]
                                  .rearrange("(a p) t -> p a t", p=128)
                                  ).then_inc(S["cl"], 16)
            elif role == "scalar":
                for it in range(16):
                    eng.wait_ge(S["cl"], 48 * (it + 1))
                    for a in range(8):
                        eng.activation(c_w[:, a * 512:(a + 1) * 512],
                                       c_g[:, a * 512:(a + 1) * 512], AF.Silu
                                       ).then_inc(S["cs1"], 1)
                        eng.wait_ge(S["cv1"], 16 * it + 2 * a + 2)
                        eng.activation(c_g[:, a * 512:(a + 1) * 512],
                                       c_w[:, a * 512:(a + 1) * 512], AF.Square
                                       ).then_inc(S["cs1"], 1)
                    eng.wait_ge(S["cm"], it + 1)
                    if it >= 1:
                        eng.wait_ge(S["cbc"], it)
                    eng.activation(stat[0:1, 0:512], pbs[0:1, 0:512], AF.Ln,
                                   scale=1.0 / 1024.0, bias=kbias[0:1, 0:1])
                    eng.activation(statb[0:1, 0:512], stat[0:1, 0:512], AF.Exp,
                                   scale=-0.5).then_inc(S["cs1"], 1)
            elif role == "vector":
                for it in range(16):
                    fg, th = it // 2, it % 2
                    for a in range(8):
                        eng.wait_ge(S["cs1"], 17 * it + 2 * a + 1)
                        eng.scalar_tensor_tensor(
                            c_y[:, a * 512:(a + 1) * 512],
                            c_h[:, a * 512:(a + 1) * 512],
                            dxsb[:, fg * 8 + a:fg * 8 + a + 1],
                            c_y[:, a * 512:(a + 1) * 512],
                            ALU.mult, ALU.add).then_inc(S["cv1"], 1)
                        eng.tensor_mul(c_w[:, a * 512:(a + 1) * 512],
                                       c_y[:, a * 512:(a + 1) * 512],
                                       c_w[:, a * 512:(a + 1) * 512]
                                       ).then_inc(S["cv1"], 1)
                    eng.wait_ge(S["cbc"], it + 1)
                    for a in range(8):
                        eng.scalar_tensor_tensor(
                            normed[:, fg * 8 + a, th * 512:(th + 1) * 512],
                            c_w[:, a * 512:(a + 1) * 512],
                            nwsb[:, fg * 8 + a:fg * 8 + a + 1],
                            pb[0][:], ALU.mult, ALU.mult).then_inc(S["cn"], 1)
            elif role == "tensor":
                for it in range(16):
                    for a in range(8):
                        eng.wait_ge(S["cs1"], 17 * it + 2 * a + 2)
                        mm = eng.matmul(pbs[0:1, 0:512], k_ones[:, 0:1],
                                        c_g[:, a * 512:(a + 1) * 512],
                                        start=(a == 0), stop=(a == 7))
                    mm.then_inc(S["cm"], 1)
                    eng.wait_ge(S["cs1"], 17 * (it + 1))
                    if it >= 1:
                        eng.wait_ge(S["cn"], 8 * it)
                    eng.matmul(pb[0][:], k_ones[0:1, :], statb[0:1, 0:512],
                               start=True, stop=True).then_inc(S["cbc"], 1)

        def d_ph(role, eng):
            if role == "gpsimd":
                for idx in range(64):
                    ot, kh = idx // 2, idx % 2
                    if idx >= 3:
                        eng.wait_ge(S["dmm"], (idx - 3) // 2 + 1)
                    eng.dma_start(
                        out=wring[idx % 3].rearrange("p (kb e) -> p kb e", kb=32),
                        in_=WoutT[kh * 4096:(kh + 1) * 4096,
                                  ot * 128:(ot + 1) * 128]
                        .rearrange("(kb p) e -> p kb e", p=128)
                        ).then_inc(S["dw"], 16)
            elif role == "sync":
                eng.wait_ge(S["est"], 16 * 32)
                for ot in range(32):
                    if ot >= 1:
                        eng.wait_ge(S["dev"], ot)
                    eng.dma_start(out=d_xc, in_=xT[ot * 128:(ot + 1) * 128, 3:1027]
                                  ).then_inc(S["dl"], 16)
                    eng.dma_start(out=d_fc, in_=ffn_scr[ot * 128:(ot + 1) * 128, :]
                                  ).then_inc(S["dl"], 16)
                    eng.wait_ge(S["dte"], ot + 1)
                    eng.dma_start(out=o_out.rearrange("(tt t) o -> t tt o", t=128)
                                  [:, :, ot * 128:(ot + 1) * 128],
                                  in_=d_ot.rearrange("p (tt o) -> p tt o", tt=8)
                                  ).then_inc(S["dst"], 16)
                eng.wait_ge(S["dst"], 16 * 32)
            elif role == "tensor":
                for ot in range(32):
                    s2 = 2 * (ot % 2)
                    for kh in range(2):
                        idx = 2 * ot + kh
                        eng.wait_ge(S["dw"], 16 * (idx + 1))
                        if ot >= 2 and kh == 0:
                            eng.wait_ge(S["dev"], ot - 1)
                        wr = wring[idx % 3].rearrange("p (kb e) -> p kb e", kb=32)
                        for k in range(32):
                            j = kh * 32 + k
                            eng.matmul(pb[s2][:], wr[:, k, :], normed[:, j, 0:512],
                                       start=(j == 0), stop=False)
                            mm = eng.matmul(pb[s2 + 1][:], wr[:, k, :],
                                            normed[:, j, 512:1024],
                                            start=(j == 0), stop=(j == 63))
                    mm.then_inc(S["dmm"], 1)
                    eng.wait_ge(S["dev"], ot + 1)
                    if ot >= 1:
                        eng.wait_ge(S["dte"], ot)
                    for tt in range(8):
                        bank = pb[4 + tt // 4].bitcast(F32) if False else pb[4 + tt // 4]
                        mm = eng.transpose(bank[:, (tt % 4) * 128:(tt % 4 + 1) * 128],
                                           d_os[:, tt * 128:(tt + 1) * 128], idf)
                    mm.then_inc(S["dtr"], 1)
            elif role == "vector":
                for ot in range(32):
                    eng.wait_ge(S["dmm"], ot + 1)
                    eng.wait_ge(S["dl"], 32 * (ot + 1))
                    if ot >= 1:
                        eng.wait_ge(S["dtr"], ot)
                    s2 = 2 * (ot % 2)
                    eng.tensor_add(d_os[:, 0:512], pb[s2][:], d_xc[:, 0:512])
                    eng.tensor_add(d_os[:, 512:1024], pb[s2 + 1][:],
                                   d_xc[:, 512:1024])
                    eng.tensor_add(d_os[:, 0:512], d_os[:, 0:512], d_fc[:, 0:512])
                    eng.tensor_add(d_os[:, 512:1024], d_os[:, 512:1024],
                                   d_fc[:, 512:1024]).then_inc(S["dev"], 1)
            elif role == "scalar":
                for ot in range(32):
                    eng.wait_ge(S["dtr"], ot + 1)
                    if ot >= 1:
                        eng.wait_ge(S["dst"], 16 * ot)
                    for tt in range(8):
                        bank = pb[4 + tt // 4]
                        ins = eng.activation(
                            d_ot[:, tt * 128:(tt + 1) * 128],
                            bank[:, (tt % 4) * 128:(tt % 4 + 1) * 128], AF.Copy)
                    ins.then_inc(S["dte"], 1)

        def dumps_ph(role, eng):
            if role != "sync":
                return
            items = []
            for name, ap in dump_aps.items():
                if name == "xn":
                    items.append((ap, ar1[:, 0:32 * TH]))
                elif name == "gate":
                    items.append((ap, gate_scr))
                elif name == "hbc":
                    items.append((ap, hbc_scr))
                elif name == "dtpack":
                    items.append((ap[:, 0:1024], f_dt[:, 0:1024]))
                    items.append((ap[:, 1024:2048], f_ac[:, 0:1024]))
                    items.append((ap[:, 2048:2056], csg[:]))
                elif name == "dtbf":
                    items.append((ap, ard))
                elif name == "vecs":
                    items.append((ap, vecs[:]))
                elif name == "fm":
                    items.append((ap, f_m[:, 0:1024]))
                elif name == "y":
                    items.append((ap, y_scr))
                elif name == "ffn":
                    items.append((ap, ffn_scr))
                elif name == "normed":
                    items.append((ap, ar1[:, 0:65536]))
                elif name == "rstate":
                    items.append((ap, r_f))
                elif name == "axc":
                    items.append((ap, e_axc[0:64, :]))
                elif name == "bcat":
                    items.append((ap, e_bcat[0:64, :]))
            cnt = 0
            for out, src in items:
                eng.dma_start(out=out, in_=src).then_inc(S["dmp"], 16)
                cnt += 1
            if cnt:
                eng.wait_ge(S["dmp"], 16 * cnt)

        ETS2 = [(18432, "dt")] + [(8192 + a * 128, "hbc") for a in range(72)]

        def q0_ph(role, eng):
            p0(role, eng, PS="q0", x_src=xT2)

        def qa_ph(role, eng):
            a_ph(role, eng, PS="q", ets=ETS2, scr=hbc2_scr, full=False,
                 gam=gamr2, dd=ddec2)

        PH_FN = {"p0": p0, "a": a_ph, "e": e_ph, "q0": q0_ph, "qa": qa_ph,
                 "qb": b1, "b2": b2, "c": c_ph, "d": d_ph}
        phases = [PH_FN[p] for p in use]

        def emit(role, eng):
            for i, ph in enumerate(phases):
                ph(role, eng)
                if role == "vector":
                    eng.tensor_copy(kbias[0:1, 0:1], kbias[0:1, 0:1]
                                    ).then_inc(S["bar"], 1)
                elif role == "scalar":
                    eng.activation(kbias[0:1, 1:2], kbias[0:1, 1:2], AF.Copy
                                   ).then_inc(S["bar"], 1)
                elif role == "gpsimd":
                    eng.memset(zrf[0:1, 0:1], 0.0).then_inc(S["bar"], 1)
                else:
                    eng.sem_inc(S["bar"], 1)
                eng.wait_ge(S["bar"], 5 * (i + 1))
            if dump_aps:
                dumps_ph(role, eng)

        @blk.sync
        def _(eng):
            emit("sync", eng)

        @blk.gpsimd
        def _(eng):
            emit("gpsimd", eng)

        @blk.tensor
        def _(eng):
            emit("tensor", eng)

        @blk.vector
        def _(eng):
            emit("vector", eng)

        @blk.scalar
        def _(eng):
            emit("scalar", eng)

    return nc


def _prep_inputs(inputs):
    f32 = np.float32
    bf = ml_dtypes.bfloat16
    x = np.asarray(inputs["L0_attn_x"], f32)
    WinT = np.ascontiguousarray(np.asarray(inputs["L0_attn_in_proj_w"], f32).T)
    WoutT = np.ascontiguousarray(np.asarray(inputs["L0_attn_out_proj_w"], f32).T)
    kern = np.ascontiguousarray(np.asarray(inputs["L0_ffn_kernel"], f32))
    lora_A = np.asarray(inputs["L0_ffn_lora_A"], f32)
    lora_B = np.asarray(inputs["L0_ffn_lora_B"], f32)
    loraAt = np.ascontiguousarray(lora_A.transpose(0, 2, 1))
    loraAt4 = np.ascontiguousarray(
        loraAt.reshape(4, 4, D, 32).transpose(0, 2, 1, 3).reshape(4, D, 128))
    loraBperm = np.ascontiguousarray(lora_B.transpose(0, 2, 1).reshape(512, D))
    loraAst = np.ascontiguousarray(np.asarray(inputs["L0_ffn_lora_A_shared"], f32).T)
    loraBst = np.ascontiguousarray(np.asarray(inputs["L0_ffn_lora_B_shared"], f32).T)
    gate = np.asarray(inputs["L0_ffn_gate"], f32)
    conv_w4 = np.ascontiguousarray(
        np.asarray(inputs["L0_attn_conv_w"], f32).reshape(80, 128, 4)
        .transpose(1, 0, 2).reshape(128, 320))
    conv_b = np.ascontiguousarray(np.asarray(inputs["L0_attn_conv_b"], f32)
                                  .reshape(80, 128).T)
    dtb = np.ascontiguousarray(np.asarray(inputs["L0_attn_dt_bias"], f32)
                               .reshape(NH, 1))
    Alog = np.ascontiguousarray(np.asarray(inputs["L0_attn_A_log"], f32)
                                .reshape(NH, 1))
    Dexp = np.ascontiguousarray(
        np.repeat(np.asarray(inputs["L0_attn_D"], f32), PH).reshape(64, 128).T)
    nwcol = np.ascontiguousarray(np.asarray(inputs["L0_attn_norm_w"], f32)
                                 .reshape(64, 128).T)
    bias_col = np.ascontiguousarray(np.asarray(inputs["L0_ffn_bias"], f32)
                                    .reshape(32, 128).T)

    eye = np.eye(128, dtype=f32)
    c_ident = eye.astype(bf)
    c_identf = eye.copy()
    c_sl = (np.arange(128)[:, None] < np.arange(128)[None, :]).astype(f32).astype(bf)
    c_negid = (NEG_BIG * eye).astype(bf)
    bi = np.zeros((16, 2048), f32)
    for j in range(16):
        bi[j, j * 128:(j + 1) * 128] = 1.0
    c_bi16 = np.ascontiguousarray(np.tile(bi, (3, 1))).astype(bf)
    c_ones = np.ones((128, 128), f32).astype(bf)
    c_id4 = np.tile(np.eye(32, dtype=f32), (4, 1)).astype(bf)
    selgm = np.zeros((16, 512), f32)
    for grp in range(4):
        for p in range(128):
            selgm[grp * 4 + p // 32, grp * 128 + p] = 1.0
    c_selg = selgm.astype(bf)

    shared = dict(WinT=WinT, WoutT=WoutT, kern=kern, loraAt4=loraAt4,
                  loraBperm=loraBperm, loraAst=loraAst, loraBst=loraBst,
                  conv_w4=conv_w4, conv_b=conv_b, dtb=dtb, Alog=Alog,
                  Dexp=Dexp, nwcol=nwcol, bias_col=bias_col,
                  c_ident=c_ident, c_identf=c_identf, c_sl=c_sl,
                  c_negid=c_negid, c_bi16=c_bi16, c_ones=c_ones,
                  c_id4=c_id4, c_selg=c_selg)

    in_maps = []
    for c in range(8):
        b, half = c // 2, c % 2
        if half == 0:
            xs = np.concatenate([np.zeros((3, D), f32), x[b, 0:T]], axis=0)
        else:
            xs = x[b, T - 3:2 * T]
        m = dict(shared)
        m["xT"] = np.ascontiguousarray(xs.T)
        xs2 = np.concatenate([np.zeros((3, D), f32), x[b, 0:T]], axis=0)
        m["xT2"] = np.ascontiguousarray(xs2.T)
        m["gate_col"] = np.ascontiguousarray(gate[c].reshape(16, 1))
        m["cmask"] = np.full((NH, 1), 1.0 - half, f32)
        m["umask"] = np.full((NH, 1), float(half), f32)
        in_maps.append(m)
    return in_maps


def kernel(**inputs):
    in_maps = _prep_inputs(inputs)
    nc = build_program()
    res = run_bass_kernel_spmd(nc, in_maps, core_ids=list(range(8)))
    out = np.empty((4, 2048, D), np.float32)
    for c in range(8):
        b, half = c // 2, c % 2
        out[b, half * T:(half + 1) * T] = res.results[c]["o_out"]
    return out


# version 05 (final)
# speedup vs baseline: 1.0326x; 1.0326x over previous
import contextlib
import numpy as np
import ml_dtypes
import concourse.bass as bass
import concourse.mybir as mybir
from concourse.bass_utils import run_bass_kernel_spmd

F32 = mybir.dt.float32
BF16 = mybir.dt.bfloat16
AF = mybir.ActivationFunctionType
ALU = mybir.AluOpType

D = 4096
ET = 18560
INTER = 8192
CONVD = 10240
NH, PH, NS, G = 128, 64, 128, 8
T = 1024
TH = 1027
NZ = 8
NEG_BIG = -1e30

ETS = [(18432, "dt")] + [(8192 + a * 128, "hbc") for a in range(80)] \
    + [(a * 128, "gate") for a in range(64)]
NET = len(ETS)
JC = [(0, 343), (343, 685), (685, 1027)]

ALL_PHASES = ["p0", "a", "e", "q0", "qa", "qb", "b2", "c", "d"]


def build_program(upto="d", dumps=()):
    nph = ALL_PHASES.index(upto) + 1
    use = ALL_PHASES[:nph]
    nc = bass.Bass("TRN2", target_bir_lowering=False, debug=False)

    def di(name, shape, dt=F32):
        return nc.dram_tensor(name, list(shape), dt, kind="ExternalInput").ap()

    xT = di("xT", (D, TH))
    xT2 = di("xT2", (D, TH))
    WinT = di("WinT", (D, ET))
    WoutT = di("WoutT", (INTER, D))
    kern = di("kern", (D, D))
    loraAt4 = di("loraAt4", (4, D, 128))
    loraBperm = di("loraBperm", (512, D))
    loraAst = di("loraAst", (D, 32))
    loraBst = di("loraBst", (32, D))
    gate_col = di("gate_col", (16, 1))
    conv_w4 = di("conv_w4", (128, 320))
    conv_b = di("conv_b", (128, 80))
    dtb_i = di("dtb", (NH, 1))
    Alog_i = di("Alog", (NH, 1))
    Dexp_i = di("Dexp", (128, 64))
    nwcol_i = di("nwcol", (128, 64))
    bias_i = di("bias_col", (128, 32))
    cmask_i = di("cmask", (NH, 1))
    umask_i = di("umask", (NH, 1))
    c_ident = di("c_ident", (128, 128), BF16)
    c_identf = di("c_identf", (128, 128), F32)
    c_sl = di("c_sl", (128, 128), BF16)
    c_negid = di("c_negid", (128, 128), BF16)
    c_bi16 = di("c_bi16", (48, 2048), BF16)
    c_ones = di("c_ones", (128, 128), BF16)
    c_id4 = di("c_id4", (128, 32), BF16)
    c_selg = di("c_selg", (16, 512), BF16)

    o_out = nc.dram_tensor("o_out", [T, D], F32, kind="ExternalOutput").ap()
    gate_scr = nc.dram_tensor("gate_scr", [INTER, T], BF16, kind="Internal").ap()
    hbc_scr = nc.dram_tensor("hbc_scr", [CONVD, T], BF16, kind="Internal").ap()
    y_scr = nc.dram_tensor("y_scr", [INTER, T], BF16, kind="Internal").ap()
    ffn_scr = nc.dram_tensor("ffn_scr", [D, T], F32, kind="Internal").ap()
    hbc2_scr = nc.dram_tensor("hbc2_scr", [9216, T], BF16, kind="Internal").ap()
    dt_scr = nc.dram_tensor("dt_scr", [896, T], BF16, kind="Internal").ap()

    DUMP_SPECS = {
        "xn": ((128, 32 * TH), BF16), "gate": ((INTER, T), BF16),
        "hbc": ((CONVD, T), BF16), "dtpack": ((128, 3072), F32),
        "dtbf": ((128, 4096), BF16),
        "y": ((INTER, T), BF16), "ffn": ((D, T), F32),
        "normed": ((128, 65536), BF16), "rstate": ((128, INTER), BF16),
        "vecs": ((128, 8), F32), "fm": ((128, 1024), F32),
        "ccout": ((128, INTER), F32), "axc": ((64, T), BF16),
        "bcat": ((64, D), BF16),
    }
    dump_aps = {}
    for name in dumps:
        shp, dt = DUMP_SPECS[name]
        dump_aps[name] = nc.dram_tensor("dump_" + name, list(shp), dt,
                                        kind="ExternalOutput").ap()

    with contextlib.ExitStack() as stx:
        E_ = stx.enter_context
        ar1 = E_(nc.sbuf_tensor("ar1", [128, 65536], BF16))
        arw = E_(nc.sbuf_tensor("arw", [128, 12288], BF16))
        arb = E_(nc.sbuf_tensor("arb", [128, 16384], BF16))
        arf = E_(nc.sbuf_tensor("arf", [128, 4352], F32))
        csm = E_(nc.sbuf_tensor("csm", [128, 512], BF16))
        ddsb = E_(nc.sbuf_tensor("ddsb", [128, 128], BF16))
        ssb = E_(nc.sbuf_tensor("ssb", [128, 256], BF16))
        gts = E_(nc.sbuf_tensor("gts", [8, 128], BF16))
        csg = E_(nc.sbuf_tensor("csg", [128, 8], F32))
        gbf = E_(nc.sbuf_tensor("gbf", [128, 8], BF16))
        cwsb = E_(nc.sbuf_tensor("cwsb", [128, 320], F32))
        cbsb = E_(nc.sbuf_tensor("cbsb", [128, 80], F32))
        vecs = E_(nc.sbuf_tensor("vecs", [128, 8], F32))
        dxsb = E_(nc.sbuf_tensor("dxsb", [128, 64], F32))
        nwsb = E_(nc.sbuf_tensor("nwsb", [128, 64], F32))
        bcsb = E_(nc.sbuf_tensor("bcsb", [128, 32], F32))
        gcsb = E_(nc.sbuf_tensor("gcsb", [16, 1], BF16))
        selg = E_(nc.sbuf_tensor("selg", [16, 512], BF16))
        i4sb = E_(nc.sbuf_tensor("i4sb", [128, 32], BF16))
        gcolf = E_(nc.sbuf_tensor("gcolf", [128, 4], F32))
        gdsb = E_(nc.sbuf_tensor("gdsb", [128, 128], BF16))
        zrf = E_(nc.sbuf_tensor("zrf", [128, 128], F32))
        kbias = E_(nc.sbuf_tensor("kbias", [128, 2], F32))
        gbc_sb = E_(nc.sbuf_tensor("gbc_sb", [128, 128], BF16))

        bi16 = arw[0:48, 9216:11264]
        bfc = ar1[:, 41056:45152]
        gamr = ar1[0:1, 46176:47200]
        gamr2 = ar1[0:1, 47200:48224]
        rowsa_t = [ar1[0:3, 28672:30720], ar1[0:3, 59392:61440]]
        l_t = [ar1[:, 16384:18432], ar1[:, 30720:32768]]
        ce_t = [ar1[:, 18432:20480], ar1[:, 57344:59392]]
        rowse = ar1[0:1, 30720:32768]
        ddec2 = arb[:, 12288:13312]
        stat = arf[0:1, 0:3081]
        statb = arf[0:1, 3081:4108].bitcast(BF16)
        asT = arb[:, 5120:6144]
        btsb = ar1[:, 45152:46176]
        idf = arf[:, 4224:4352]
        ard = arb[:, 8192:12288]

        xn = ar1[:, 0:32 * TH].rearrange("p (kb t) -> p kb t", kb=32)
        hsT = [ar1[:, 0:8192], ar1[:, 8192:16384]]
        l_sb = ar1[:, 16384:18432]
        ce_sb = ar1[:, 18432:20480]
        yst = ar1[:, 20480:28672]
        xchd = ar1[:, 32864:41056]
        b1hsT = [arb[:, 0:8192], arb[:, 0:8192]]
        r_f = ar1[:, 49152:57344]
        normed = ar1[:, 0:65536].rearrange("p (j t) -> p j t", j=64)

        wring = [arw[:, i * 4096:(i + 1) * 4096] for i in range(3)]
        w6 = [arw[:, i * 2048:(i + 1) * 2048] for i in range(6)]
        colvst = arw[0:48, 8192:9216]

        xf_st = arb[:, 0:2054].bitcast(F32)
        xf_t = [arb[:, 0:2054].bitcast(F32), arb[:, 8192:10246].bitcast(F32)]
        cvst = [arb[:, 2054 + i * 1027:2054 + (i + 1) * 1027] for i in range(2)]
        cvo = [arb[:, 4108 + i * 1024:4108 + (i + 1) * 1024] for i in range(2)]
        gst = [arb[:, 12288 + i * 1024:12288 + (i + 1) * 1024] for i in range(2)]
        sq_st = arb[:, 6156:7183]
        c_y = arb[:, 0:4096]
        c_g = arb[:, 4096:8192]
        c_h = arb[:, 8192:12288]
        c_w = arb[:, 12288:16384]
        e_bcat = arb[:, 0:4096]
        e_axc = arb[:, 4096:5120]
        e_axall = ar1[:, 32864:36960].rearrange("p (g t) -> p g t", g=4)

        f_dtr = arf[:, 0:1027]
        f_e = arf[:, 1027:2054]
        f_m = arf[:, 2054:3078]
        f_dt = arf[:, 3078:4102]
        f_ac = arf[:, 0:1024]
        d_xc = arf[:, 0:1024]
        d_fc = arf[:, 1024:2048]
        d_os = arf[:, 2048:3072]
        d_ot = arf[:, 3072:4096]
        e_st = [arf[:, 0:1024], arf[:, 1024:2048]]

        ard0 = ard[:, 0:1024]
        ard1 = ard[:, 1024:2048]
        b_expa = ard[:, 2048:3072]
        b_ddec = ard[:, 3072:4096]

        k_id = csm[:, 0:128]
        k_sl = csm[:, 128:256]
        k_negid = csm[:, 256:384]
        k_ones = csm[:, 384:512]

        pb = [E_(nc.psum_tensor(f"pb{i}", [128, 512], F32)) for i in range(6)]
        pbt = E_(nc.psum_tensor("pbt", [128, 1024], BF16))
        pbs = E_(nc.psum_tensor("pbs", [128, 512], F32))

        sem_names = [
            "bar",
            "p0x", "p0b", "p0m", "p0s", "p0c", "kc", "cst",
            "aw", "amm", "apc", "acnv", "aslu", "ast", "adt", "adtv",
            "agt", "age", "agr",
            "b1l", "b1t", "b1e", "b1x", "b1s", "b1r", "b1g", "b1gv", "b1c", "ccd",
            "q0x", "q0b", "q0m", "q0s", "q0c",
            "qw", "qmm", "qpc", "qcnv", "qslu", "qst", "qdt", "qdtv",
            "qgt", "qge", "qgr",
            "elt", "elb", "ele", "elv", "elp", "ew", "emm", "eev", "est",
            "b2l", "b2t", "b2e", "b2x", "b2s", "b2r", "b2g", "b2gv",
            "b2sc", "b2scv", "b2d", "b2L", "b2eb", "b2ea",
            "b2G", "b2ce", "b2y", "b2ye", "b2ys", "b2r0", "b2rs",
            "dts", "cl", "cs1", "cv1", "cm", "cbc", "cn",
            "dw", "dl", "dmm", "dev", "dtr", "dte", "dst",
            "dmp",
        ]
        S = {n: E_(nc.semaphore(n)) for n in sem_names}

        blk = E_(nc.Block())

        def p0(role, eng, PS="p0", x_src=None):
            x_src = xT if x_src is None else x_src
            if role == "sync":
                consts = [] if PS != "p0" else [(k_id, c_ident), (k_sl, c_sl),
                                  (k_negid, c_negid), (k_ones, c_ones),
                                  (i4sb[:], c_id4), (selg[:], c_selg),
                                  (idf, c_identf), (cwsb[:], conv_w4),
                                  (cbsb[:], conv_b), (vecs[:, 0:1], dtb_i),
                                  (vecs[:, 1:2], Alog_i), (vecs[:, 3:4], cmask_i),
                                  (vecs[:, 4:5], umask_i), (dxsb[:], Dexp_i),
                                  (nwsb[:], nwcol_i), (bcsb[:], bias_i)]
                for dst, srcv in consts:
                    eng.dma_start(out=dst, in_=srcv).then_inc(S["cst"], 16)
                for kb in range(32):
                    if kb >= 2:
                        eng.wait_ge(S[PS + "b"], 2 * (kb - 1))
                    eng.dma_start(out=xf_t[kb % 2],
                                  in_=x_src[kb * 128:(kb + 1) * 128, :]
                                  ).then_inc(S[PS + "x"], 16)
                if PS == "p0":
                    eng.wait_ge(S["cst"], 16 * 17)
            elif role == "gpsimd":
                if PS == "p0":
                    eng.dma_start(out=gcsb[:], in_=gate_col).then_inc(S["cst"], 16)
                    eng.memset(zrf[:], 0.0)
                    eng.memset(kbias[:, 0:1], 1e-5)
                    eng.memset(kbias[:, 1:2], 1.0).then_inc(S["kc"], 1)
            elif role == "scalar":
                for kb in range(32):
                    eng.wait_ge(S[PS + "x"], 16 * (kb + 1))
                    if kb >= 2:
                        eng.wait_ge(S[PS + "m"], kb - 1)
                    eng.activation(xn[:, kb, :], xf_t[kb % 2], AF.Copy
                                   ).then_inc(S[PS + "b"], 1)
                    eng.activation(sq_st, xf_t[kb % 2], AF.Square
                                   ).then_inc(S[PS + "b"], 1)
                eng.wait_ge(S[PS + "s"], 1)
                if PS == "p0":
                    eng.wait_ge(S["kc"], 1)
                eng.activation(stat[0:1, 2 * TH:3 * TH], stat[0:1, TH:2 * TH],
                               AF.Ln, bias=kbias[0:1, 0:1])
                eng.activation(stat[0:1, 2 * TH:3 * TH], stat[0:1, 2 * TH:3 * TH],
                               AF.Exp, scale=-0.5)
                eng.activation(statb[0:1, 0:TH], stat[0:1, 0:TH], AF.Copy)
                eng.activation(statb[0:1, TH:2 * TH], stat[0:1, 2 * TH:3 * TH],
                               AF.Copy).then_inc(S[PS + "s"], 1)
            elif role == "tensor":
                for kb in range(32):
                    eng.wait_ge(S[PS + "b"], 2 * (kb + 1))
                    for j, (j0, j1) in enumerate(JC):
                        w = j1 - j0
                        eng.matmul(pb[j][0:1, 0:w], k_ones[:, 0:1], xn[:, kb, j0:j1],
                                   start=(kb == 0), stop=(kb == 31))
                        mm = eng.matmul(pb[3 + j][0:1, 0:w], k_ones[:, 0:1],
                                        sq_st[:, j0:j1], start=(kb == 0),
                                        stop=(kb == 31))
                    mm.then_inc(S[PS + "m"], 1)
                eng.wait_ge(S[PS + "s"], 2)
                for j, (j0, j1) in enumerate(JC):
                    w = j1 - j0
                    eng.matmul(pb[j][:, 0:w], k_ones[0:1, :], statb[0:1, j0:j1],
                               start=True, stop=True)
                    mm = eng.matmul(pb[3 + j][:, 0:w], k_ones[0:1, :],
                                    statb[0:1, TH + j0:TH + j1], start=True, stop=True)
                mm.then_inc(S[PS + "c"], 1)
            elif role == "vector":
                eng.wait_ge(S[PS + "m"], 32)
                for j, (j0, j1) in enumerate(JC):
                    w = j1 - j0
                    eng.tensor_scalar(stat[0:1, j0:j1], pb[j][0:1, 0:w],
                                      1.0 / D, None, ALU.mult)
                    eng.tensor_scalar(stat[0:1, TH + j0:TH + j1], pb[3 + j][0:1, 0:w],
                                      1.0 / D, None, ALU.mult)
                eng.tensor_mul(stat[0:1, 2 * TH:3 * TH], stat[0:1, 0:TH],
                               stat[0:1, 0:TH])
                eng.tensor_sub(stat[0:1, TH:2 * TH], stat[0:1, TH:2 * TH],
                               stat[0:1, 2 * TH:3 * TH]).then_inc(S[PS + "s"], 1)
                eng.wait_ge(S[PS + "c"], 1)
                for kb in range(32):
                    for j, (j0, j1) in enumerate(JC):
                        w = j1 - j0
                        eng.tensor_sub(xn[:, kb, j0:j1], xn[:, kb, j0:j1],
                                       pb[j][:, 0:w])
                        eng.tensor_mul(xn[:, kb, j0:j1], xn[:, kb, j0:j1],
                                       pb[3 + j][:, 0:w])

        def a_ph(role, eng, PS="a", ets=ETS, scr=hbc_scr, full=True,
                 gam=gamr, dd=b_ddec):
            nh = sum(1 for _, kk in ets if kk == "hbc")
            if role == "gpsimd":
                for i, (e0, knd) in enumerate(ets):
                    if i >= 3:
                        eng.wait_ge(S[PS + "mm"], i - 2)
                    eng.dma_start(
                        out=wring[i % 3].rearrange("p (kb e) -> p kb e", kb=32),
                        in_=WinT[:, e0:e0 + 128].rearrange("(kb p) e -> p kb e", p=128)
                        ).then_inc(S[PS + "w"], 16)
            elif role == "tensor":
                for i, (e0, knd) in enumerate(ets):
                    eng.wait_ge(S[PS + "w"], 16 * (i + 1))
                    if i >= 2:
                        eng.wait_ge(S[PS + "pc"], i - 1)
                    s3 = 3 * (i % 2)
                    wr = wring[i % 3].rearrange("p (kb e) -> p kb e", kb=32)
                    for k in range(32):
                        for j, (j0, j1) in enumerate(JC):
                            mm = eng.matmul(pb[s3 + j][:, 0:j1 - j0], wr[:, k, :],
                                            xn[:, k, j0:j1], start=(k == 0),
                                            stop=(k == 31))
                    mm.then_inc(S[PS + "mm"], 1)
                eng.wait_ge(S[PS + "dt"], 5)
                eng.transpose(pbt[0:8, 0:128], gbf[:], k_id).then_inc(S[PS + "gt"], 1)
            elif role == "vector":
                eng.wait_ge(S[PS + "mm"], 1)
                for j, (j0, j1) in enumerate(JC):
                    ins = eng.tensor_copy(f_dtr[:, j0:j1], pb[j][:, 0:j1 - j0])
                ins.then_inc(S[PS + "pc"], 1)
                eng.tensor_scalar(f_e[:, 0:1024], f_dtr[:, 3:1027], vecs[:, 0:1],
                                  None, ALU.add)
                eng.tensor_scalar(f_m[:, 0:1024], f_e[:, 0:1024], 0.0, None,
                                  ALU.max).then_inc(S[PS + "dtv"], 1)
                eng.wait_ge(S[PS + "dt"], 3)
                eng.tensor_add(f_dt[:, 0:1024], f_m[:, 0:1024], f_dtr[:, 0:1024])
                eng.tensor_scalar(f_dt[:, 0:1024], f_dt[:, 0:1024], 100.0, None,
                                  ALU.min).then_inc(S[PS + "dtv"], 1)
                eng.wait_ge(S[PS + "dt"], 4)
                eng.tensor_scalar(f_m[:, 0:1024], f_dt[:, 0:1024], vecs[:, 5:6],
                                  None, ALU.mult)
                for z in range(NZ):
                    eng.tensor_tensor_scan(f_ac[:, z * 128:(z + 1) * 128],
                                           f_m[:, z * 128:(z + 1) * 128], zrf[:],
                                           0.0, ALU.add, ALU.add)
                for z in range(NZ):
                    ins = eng.tensor_copy(csg[:, z:z + 1],
                                          f_ac[:, z * 128 + 127:z * 128 + 128])
                ins.then_inc(S[PS + "dtv"], 1)
                eng.wait_ge(S[PS + "dtv"], 3)
                for z in range(NZ):
                    ins = eng.tensor_scalar(f_e[:, z * 128:(z + 1) * 128],
                                            f_ac[:, z * 128:(z + 1) * 128],
                                            csg[:, z:z + 1], None, ALU.subtract)
                ins.then_inc(S[PS + "dtv"], 1)
                eng.wait_ge(S[PS + "dt"], 6)
                eng.tensor_mul(dd[:], f_dt[:, 0:1024], f_m[:, 0:1024])
                if full:
                    eng.wait_ge(S[PS + "dt"], 7)
                    eng.tensor_scalar(f_e[:, 0:1024], f_e[:, 0:1024], NEG_BIG,
                                      None, ALU.max)
                    eng.tensor_sub(f_dt[:, 0:1024], f_e[:, 0:1024],
                                   f_ac[:, 0:1024])
                    eng.tensor_copy(ard0[:], f_dt[:, 0:1024]
                                    ).then_inc(S[PS + "dtv"], 1)
                    eng.tensor_sub(f_m[:, 0:1024], f_dt[:, 0:1024], ard0[:])
                    eng.tensor_copy(ard1[:], f_m[:, 0:1024]
                                    ).then_inc(S[PS + "dtv"], 1)
                    eng.tensor_sub(f_dt[:, 0:1024], f_m[:, 0:1024], ard1[:])
                    eng.wait_ge(S["dts"], 16)
                    eng.tensor_copy(ard0[:], f_dt[:, 0:1024]
                                    ).then_inc(S[PS + "dtv"], 1)
                    eng.wait_ge(S["dts"], 32)
                    eng.tensor_copy(ard1[:], f_ac[:, 0:1024]
                                    ).then_inc(S[PS + "dtv"], 1)
                    eng.tensor_sub(f_m[:, 0:1024], f_ac[:, 0:1024], ard1[:])
                    eng.wait_ge(S["dts"], 48)
                    eng.tensor_copy(ard0[:], f_m[:, 0:1024]
                                    ).then_inc(S[PS + "dtv"], 1)
                    eng.tensor_sub(f_dt[:, 0:1024], f_m[:, 0:1024], ard0[:])
                    eng.wait_ge(S["dts"], 64)
                    eng.tensor_copy(ard1[:], f_dt[:, 0:1024]
                                    ).then_inc(S[PS + "dtv"], 1)
                for a in range(nh):
                    i = 1 + a
                    eng.wait_ge(S[PS + "pc"], i + 1)
                    if a >= 2:
                        eng.wait_ge(S[PS + "slu"], a - 1)
                    cv = cvst[a % 2]
                    co = cvo[a % 2]
                    eng.tensor_scalar(co[:], cv[:, 0:1024],
                                      cwsb[:, a * 4:a * 4 + 1], None, ALU.mult)
                    for kk in range(1, 4):
                        ins = eng.scalar_tensor_tensor(
                            co[:], cv[:, kk:kk + 1024],
                            cwsb[:, a * 4 + kk:a * 4 + kk + 1],
                            co[:], ALU.mult, ALU.add)
                    ins.then_inc(S[PS + "cnv"], 1)
            elif role == "scalar":
                eng.wait_ge(S[PS + "dtv"], 1)
                eng.activation(f_dtr[:, 0:1024], f_e[:, 0:1024], AF.Abs)
                eng.activation(f_e[:, 0:1024], f_dtr[:, 0:1024], AF.Exp, scale=-1.0)
                eng.activation(f_dtr[:, 0:1024], f_e[:, 0:1024], AF.Ln,
                               bias=kbias[:, 1:2]).then_inc(S[PS + "dt"], 3)
                eng.activation(vecs[:, 2:3], vecs[:, 1:2], AF.Exp)
                eng.activation(vecs[:, 5:6], vecs[:, 2:3], AF.Copy, scale=-1.0
                               ).then_inc(S[PS + "dt"], 1)
                eng.wait_ge(S[PS + "dtv"], 3)
                eng.activation(gbf[:], csg[:], AF.Exp).then_inc(S[PS + "dt"], 1)
                eng.wait_ge(S[PS + "dtv"], 4)
                eng.activation(f_m[:, 0:1024], f_e[:, 0:1024], AF.Exp, scale=-1.0
                               ).then_inc(S[PS + "dt"], 1)
                eng.activation(f_e[:, 0:1024], f_dt[:, 0:1024], AF.Ln
                               ).then_inc(S[PS + "dt"], 1)

                for i, (e0, knd) in enumerate(ets):
                    if knd == "dt":
                        continue
                    s3 = 3 * (i % 2)
                    eng.wait_ge(S[PS + "mm"], i + 1)
                    if knd == "hbc":
                        a = i - 1
                        if a >= 2:
                            eng.wait_ge(S[PS + "st"], 16 * (a - 1))
                        cv = cvst[a % 2]
                        for j, (j0, j1) in enumerate(JC):
                            ins = eng.activation(cv[:, j0:j1],
                                                 pb[s3 + j][:, 0:j1 - j0], AF.Copy)
                        ins.then_inc(S[PS + "pc"], 1)
                        eng.wait_ge(S[PS + "cnv"], a + 1)
                        eng.activation(cv[:, 0:1024], cvo[a % 2][:], AF.Silu,
                                       bias=cbsb[:, a:a + 1]).then_inc(S[PS + "slu"], 1)
                    else:
                        gi = i - 1 - nh
                        if gi >= 2:
                            eng.wait_ge(S[PS + "st"], 16 * (nh + gi - 1))
                        gs = gst[gi % 2]
                        eng.activation(gs[:, 0:340], pb[s3 + 0][:, 3:343], AF.Copy)
                        eng.activation(gs[:, 340:682], pb[s3 + 1][:, 0:342], AF.Copy)
                        eng.activation(gs[:, 682:1024], pb[s3 + 2][:, 0:342], AF.Copy
                                       ).then_inc(S[PS + "pc"], 1)
                eng.wait_ge(S[PS + "gt"], 1)
                eng.activation(gts[:], pbt[0:8, 0:128], AF.Copy).then_inc(S[PS + "ge"], 1)
            elif role == "sync":
                if full:
                    for pi, (sem_v, ab) in enumerate(
                            [(5, 0), (6, 1), (7, 0), (8, 1), (9, 0), (10, 1)]):
                        eng.wait_ge(S[PS + "dtv"], sem_v)
                        eng.dma_start(out=dt_scr[pi * 128:(pi + 1) * 128, :],
                                      in_=(ard0 if ab == 0 else ard1)
                                      ).then_inc(S["dts"], 16)

                for a in range(nh):
                    eng.wait_ge(S[PS + "slu"], a + 1)
                    eng.dma_start(out=scr[a * 128:(a + 1) * 128, :],
                                  in_=cvst[a % 2][:, 0:1024]).then_inc(S[PS + "st"], 16)
                if full:
                    for gi in range(64):
                        eng.wait_ge(S[PS + "pc"], 2 + nh + gi)
                        eng.dma_start(out=gate_scr[gi * 128:(gi + 1) * 128, :],
                                      in_=gst[gi % 2][:]).then_inc(S[PS + "st"], 16)
                eng.wait_ge(S[PS + "ge"], 1)
                eng.dma_start(out=gam.rearrange("o (z h) -> o z h", z=8),
                              in_=gts[:]).then_inc(S[PS + "gr"], 16)

        def b_transposes(eng, P, z, dd=None):
            dd = b_ddec if dd is None else dd
            eng.wait_ge(S[P + "e"], 2 * z)
            eng.transpose(pbt[:, 0:128], dd[:, z * 128:(z + 1) * 128], k_id
                          ).then_inc(S[P + "t"], 1)
            eng.wait_ge(S[P + "e"], 2 * z + 1)
            for g in range(G):
                mm = eng.transpose(pbt[:, g * 128:(g + 1) * 128],
                                   bfc[:, (z % 2) * 2048 + g * 128:
                                       (z % 2) * 2048 + (g + 1) * 128], k_id)
            mm.then_inc(S[P + "t"], 1)

        def b_tr_evac(eng, P, z):
            eng.wait_ge(S[P + "t"], 2 * z + 1)
            if z >= 1:
                eng.wait_ge(S[P + "x"], z)
            eng.activation(ddsb[:], pbt[:, 0:128], AF.Copy).then_inc(S[P + "e"], 1)
            eng.wait_ge(S[P + "t"], 2 * z + 2)
            if z >= 1:
                eng.wait_ge(S[P + "s"], 8 * z)
            eng.activation(btsb[:], pbt[:], AF.Copy).then_inc(S[P + "e"], 1)

        def b_xchd(eng, P, z, hT, nload):
            eng.wait_ge(S[P + "l"], nload * (z + 1))
            eng.wait_ge(S[P + "e"], 2 * z + 1)
            if z >= 1:
                eng.wait_ge(S[P + "s"], 8 * z)
            eng.tensor_tensor(
                xchd.rearrange("p (h pp) -> p h pp", h=128),
                hT[z % 2].rearrange("p (h pp) -> p h pp", h=128),
                ddsb[:, :, None].broadcast_to([128, 128, PH]),
                ALU.mult).then_inc(S[P + "x"], 1)

        def b_states_mm(eng, P, z, g, off):
            for hh in range(16):
                h = g * 16 + hh
                bank = pb[off + hh // 8]
                mm = eng.matmul(bank[:, (hh % 8) * 64:(hh % 8 + 1) * 64],
                                btsb[:, g * 128:(g + 1) * 128],
                                xchd[:, h * 64:(h + 1) * 64], start=True, stop=True)
            mm.then_inc(S[P + "s"], 1)

        def b_gbc_mm(eng, P, z, gam=None):
            gam = gamr if gam is None else gam
            if z > 0:
                eng.wait_ge(S[P + "gv"], z)
                if P == "b2":
                    eng.wait_ge(S["b2scv"], 8 * z)
            eng.matmul(pbs[:, 128:256], k_ones[0:1, :],
                       gam[0:1, z * 128:(z + 1) * 128],
                       start=True, stop=True).then_inc(S[P + "g"], 1)

        def b_gbc_evac(eng, P, z):
            eng.wait_ge(S[P + "g"], z + 1)
            if z > 0:
                eng.wait_ge(S[P + "r"], 8 * z)
            eng.activation(gbc_sb[:], pbs[:, 128:256], AF.Copy
                           ).then_inc(S[P + "gv"], 1)

        def b_rupd_g(eng, P, z, g, off, with_y):
            zg = 8 * z + g
            eng.wait_ge(S[P + "gv"], z + 1)
            if with_y:
                eng.wait_ge(S["b2y"], zg + 1)
            eng.wait_ge(S[P + "s"], zg + 1)
            rv = r_f[:, g * 1024:(g + 1) * 1024].rearrange(
                "p (h pp) -> p h pp", h=16)
            eng.tensor_tensor(rv, rv,
                              gbc_sb[:, g * 16:(g + 1) * 16, None]
                              .broadcast_to([128, 16, PH]), ALU.mult)
            eng.tensor_add(r_f[:, g * 1024:g * 1024 + 512],
                           r_f[:, g * 1024:g * 1024 + 512], pb[off][:])
            eng.tensor_add(r_f[:, g * 1024 + 512:(g + 1) * 1024],
                           r_f[:, g * 1024 + 512:(g + 1) * 1024],
                           pb[off + 1][:]).then_inc(S[P + "r"], 1)

        def b1(role, eng):
            if role == "sync":
                eng.wait_ge(S["qgr"], 16)
                eng.wait_ge(S["qst"], 16 * 72)
                for z in range(NZ):
                    if z >= 1:
                        eng.wait_ge(S["b1x"], z)
                    eng.dma_start_transpose(b1hsT[z % 2],
                                            hbc2_scr[0:INTER, z * 128:(z + 1) * 128]
                                            ).then_inc(S["b1l"], 16)
                    if z >= 2:
                        eng.wait_ge(S["b1s"], 8 * (z - 1))
                    eng.dma_start(out=bfc[:, (z % 2) * 2048:(z % 2) * 2048 + 1024]
                                  .rearrange("p (g s) -> p g s", g=8),
                                  in_=hbc2_scr[INTER:INTER + 1024,
                                               z * 128:(z + 1) * 128]
                                  .rearrange("(g n) s -> n g s", n=128)
                                  ).then_inc(S["b1l"], 16)
            elif role == "tensor":
                for z in range(NZ):
                    eng.wait_ge(S["b1l"], 32 * (z + 1))
                    b_transposes(eng, "b1", z, dd=ddec2)
                    b_gbc_mm(eng, "b1", z, gam=gamr2)
                    eng.wait_ge(S["b1x"], z + 1)
                    eng.wait_ge(S["b1e"], 2 * (z + 1))
                    for g in range(G):
                        if g >= 2:
                            eng.wait_ge(S["b1r"], 8 * z + g - 1)
                        b_states_mm(eng, "b1", z, g, 2 * (g % 2))
            elif role == "scalar":
                for z in range(NZ):
                    b_tr_evac(eng, "b1", z)
                    b_gbc_evac(eng, "b1", z)
            elif role == "vector":
                eng.memset(r_f, 0.0)
                for z in range(NZ):
                    b_xchd(eng, "b1", z, b1hsT, 32)
                    for g in range(G):
                        b_rupd_g(eng, "b1", z, g, 2 * (g % 2), False)
                eng.tensor_scalar(r_f, r_f, vecs[:, 4:5], None, ALU.mult)

        def e_ph(role, eng):
            if role == "gpsimd":
                eng.dma_start(out=asT.rearrange("p (kb r) -> p kb r", kb=32),
                              in_=loraAst.rearrange("(kb p) r -> p kb r", p=128)
                              ).then_inc(S["elp"], 16)
                eng.dma_start(out=arb[32:64, 0:4096], in_=loraBst
                              ).then_inc(S["elp"], 16)
                for grp in range(4):
                    if grp == 3:
                        eng.wait_ge(S["elt"], 2)
                    eng.dma_start(
                        out=wring[grp % 3].rearrange("p (kb r) -> p kb r", kb=32),
                        in_=loraAt4[grp].rearrange("(kb p) r -> p kb r", p=128)
                        ).then_inc(S["elp"], 16)
                for idx in range(8):
                    eng.wait_ge(S["elt"], 5)
                    if idx >= 6:
                        eng.wait_ge(S["elb"], idx - 5)
                    oh, grp = idx // 4, idx % 4
                    eng.dma_start(out=w6[idx % 6][:, 0:2048],
                                  in_=loraBperm[grp * 128:(grp + 1) * 128,
                                                oh * 2048:(oh + 1) * 2048]
                                  ).then_inc(S["elp"], 16)
                for ot in range(32):
                    if ot < 3:
                        eng.wait_ge(S["elb"], 8)
                    else:
                        eng.wait_ge(S["emm"], ot - 2)
                    eng.dma_start(
                        out=wring[ot % 3].rearrange("p (kb e) -> p kb e", kb=32),
                        in_=kern[:, ot * 128:(ot + 1) * 128]
                        .rearrange("(kb p) e -> p kb e", p=128)
                        ).then_inc(S["ew"], 16)
            elif role == "tensor":
                for grp in range(4):
                    mm = eng.matmul(pbs[:, 384 + grp:385 + grp],
                                    selg[:, grp * 128:(grp + 1) * 128], gcsb[:],
                                    start=True, stop=True)
                mm.then_inc(S["elt"], 1)
                for grp in range(4):
                    eng.wait_ge(S["elp"], 16 * (3 + grp))
                    if grp >= 2:
                        eng.wait_ge(S["ele"], grp - 1)
                    wr = wring[grp % 3].rearrange("p (kb r) -> p kb r", kb=32)
                    pA, pB = pb[2 * (grp % 2)], pb[2 * (grp % 2) + 1]
                    for k in range(32):
                        eng.matmul(pA[:], wr[:, k, :], xn[:, k, 3:515],
                                   start=(k == 0), stop=(k == 31))
                        mm = eng.matmul(pB[:], wr[:, k, :], xn[:, k, 515:1027],
                                        start=(k == 0), stop=(k == 31))
                    mm.then_inc(S["elt"], 1)
                eng.wait_ge(S["elv"], 2)
                eng.wait_ge(S["ele"], 4)
                for grp in range(4):
                    eng.matmul(pb[4][0:32, :], gdsb[:, grp * 32:(grp + 1) * 32],
                               e_axall[:, grp, 0:512], start=(grp == 0),
                               stop=(grp == 3))
                    eng.matmul(pb[5][0:32, :], gdsb[:, grp * 32:(grp + 1) * 32],
                               e_axall[:, grp, 512:1024], start=(grp == 0),
                               stop=(grp == 3))
                eng.wait_ge(S["elp"], 16)
                aT = asT.rearrange("p (kb r) -> p kb r", kb=32)
                for k in range(32):
                    eng.matmul(pb[4][32:64, :], aT[:, k, :], xn[:, k, 3:515],
                               start=(k == 0), stop=(k == 31))
                    mm = eng.matmul(pb[5][32:64, :], aT[:, k, :], xn[:, k, 515:1027],
                                    start=(k == 0), stop=(k == 31))
                mm.then_inc(S["elt"], 1)
                for idx in range(8):
                    eng.wait_ge(S["elp"], 16 * (6 + idx + 1))
                    oh, grp = idx // 4, idx % 4
                    if idx == 4:
                        eng.wait_ge(S["ele"], 6)
                    for jj in range(4):
                        mm = eng.matmul(pb[jj][0:32, :],
                                        gdsb[:, grp * 32:(grp + 1) * 32],
                                        w6[idx % 6][:, jj * 512:(jj + 1) * 512],
                                        start=(grp == 0), stop=(grp == 3))
                    mm.then_inc(S["elb"], 1)
                eng.wait_ge(S["ele"], 7)
                for ot in range(32):
                    eng.wait_ge(S["ew"], 16 * (ot + 1))
                    if ot >= 2:
                        eng.wait_ge(S["eev"], ot - 1)
                    s2 = 2 * (ot % 2)
                    wr = wring[ot % 3].rearrange("p (kb e) -> p kb e", kb=32)
                    for k in range(32):
                        eng.matmul(pb[s2][:], wr[:, k, :], xn[:, k, 3:515],
                                   start=(k == 0), stop=False)
                        eng.matmul(pb[s2 + 1][:], wr[:, k, :], xn[:, k, 515:1027],
                                   start=(k == 0), stop=False)
                    eng.matmul(pb[s2][:], e_bcat[0:64, ot * 128:(ot + 1) * 128],
                               e_axc[0:64, 0:512], start=False, stop=True)
                    eng.matmul(pb[s2 + 1][:], e_bcat[0:64, ot * 128:(ot + 1) * 128],
                               e_axc[0:64, 512:1024], start=False, stop=True
                               ).then_inc(S["emm"], 1)
            elif role == "vector":
                eng.wait_ge(S["elt"], 1)
                eng.tensor_copy(gcolf[:], pbs[:, 384:388]).then_inc(S["elv"], 1)
                eng.wait_ge(S["elv"], 1)
                for grp in range(4):
                    ins = eng.tensor_scalar(gdsb[:, grp * 32:(grp + 1) * 32],
                                            i4sb[:], gcolf[:, grp:grp + 1], None,
                                            ALU.mult)
                ins.then_inc(S["elv"], 1)
                for ot in range(32):
                    eng.wait_ge(S["emm"], ot + 1)
                    if ot >= 2:
                        eng.wait_ge(S["est"], 16 * (ot - 1))
                    s2 = 2 * (ot % 2)
                    eng.tensor_scalar(e_st[ot % 2][:, 0:512], pb[s2][:],
                                      bcsb[:, ot:ot + 1], None, ALU.add)
                    eng.tensor_scalar(e_st[ot % 2][:, 512:1024], pb[s2 + 1][:],
                                      bcsb[:, ot:ot + 1], None, ALU.add
                                      ).then_inc(S["eev"], 1)
            elif role == "scalar":
                for grp in range(4):
                    eng.wait_ge(S["elt"], 2 + grp)
                    eng.activation(e_axall[:, grp, 0:512], pb[2 * (grp % 2)][:],
                                   AF.Copy)
                    eng.activation(e_axall[:, grp, 512:1024],
                                   pb[2 * (grp % 2) + 1][:], AF.Copy
                                   ).then_inc(S["ele"], 1)
                eng.wait_ge(S["elt"], 6)
                eng.activation(e_axc[0:64, 0:512], pb[4][0:64, :], AF.Copy)
                eng.activation(e_axc[0:64, 512:1024], pb[5][0:64, :], AF.Copy
                               ).then_inc(S["ele"], 1)
                for oh in range(2):
                    eng.wait_ge(S["elb"], 4 * (oh + 1))
                    for jj in range(4):
                        ins = eng.activation(
                            e_bcat[0:32, oh * 2048 + jj * 512:
                                   oh * 2048 + (jj + 1) * 512],
                            pb[jj][0:32, :], AF.Copy)
                    ins.then_inc(S["ele"], 1)
            elif role == "sync":
                for ot in range(32):
                    eng.wait_ge(S["eev"], ot + 1)
                    eng.dma_start(out=ffn_scr[ot * 128:(ot + 1) * 128, :],
                                  in_=e_st[ot % 2][:]).then_inc(S["est"], 16)

        def b2(role, eng):
            zs = lambda z: slice(z * 128, (z + 1) * 128)
            if role == "sync":
                eng.dma_start(out=bi16, in_=c_bi16).then_inc(S["agr"], 16)
                eng.wait_ge(S["dts"], 96)
                for z in range(NZ):
                    if z >= 2:
                        eng.wait_ge(S["b2y"], 8 * (z - 1))
                        eng.wait_ge(S["b2x"], z - 1)
                    eng.dma_start_transpose(hsT[z % 2], hbc_scr[0:INTER, zs(z)]
                                            ).then_inc(S["b2l"], 16)
                    if z >= 2:
                        eng.wait_ge(S["b2s"], 8 * (z - 1))
                        eng.wait_ge(S["b2ce"], 8 * (z - 1))
                    eng.dma_start(out=bfc[:, (z % 2) * 2048:(z % 2) * 2048 + 2048]
                                  .rearrange("p (g s) -> p g s", g=16),
                                  in_=hbc_scr[INTER:CONVD, zs(z)]
                                  .rearrange("(g n) s -> n g s", n=128)
                                  ).then_inc(S["b2l"], 16)
                    if z >= 1:
                        eng.wait_ge(S["b2d"], 8 * z)
                    for q in range(3):
                        eng.dma_start(out=colvst[q * 16:(q + 1) * 16, :]
                                      .rearrange("h (g s) -> h g s", g=8),
                                      in_=dt_scr[q * 128:(q + 1) * 128, zs(z)]
                                      .rearrange("(g h) l -> h g l", h=16)
                                      ).then_inc(S["b2l"], 16)
                    for g in range(G):
                        zg = 8 * z + g
                        if zg >= 2:
                            eng.wait_ge(S["b2d"], zg - 1)
                            eng.wait_ge(S["b2eb"], zg - 1)
                        eng.dma_start(out=rowsa_t[zg % 2]
                                      .rearrange("o (h l) -> o h l", h=16),
                                      in_=dt_scr[384:768, zs(z)]
                                      .rearrange("(q gh) l -> q gh l", gh=128)
                                      [:, g * 16:(g + 1) * 16, :]
                                      ).then_inc(S["b2rs"], 16)
                    eng.wait_ge(S["b2ye"], 16 * (z + 1))
                    eng.dma_start(out=y_scr.rearrange("(j p) t -> p j t", p=128)
                                  [:, :, zs(z)],
                                  in_=yst.rearrange("p (j l) -> p j l", j=64)
                                  ).then_inc(S["b2ys"], 16)
            elif role == "tensor":
                eng.wait_ge(S["agr"], 32)
                for z in range(NZ):
                    eng.wait_ge(S["b2l"], 80 * (z + 1))
                    b_transposes(eng, "b2", z)
                    b_gbc_mm(eng, "b2", z)
                    eng.wait_ge(S["b2e"], 2 * (z + 1))
                    eng.wait_ge(S["b2gv"], z + 1)
                    for g in range(G):
                        zg = 8 * z + g
                        if zg >= 1:
                            eng.wait_ge(S["b2scv"], zg)
                        eng.matmul(pbs[:, 0:128],
                                   bfc[:, (z % 2) * 2048 + g * 128:
                                       (z % 2) * 2048 + (g + 1) * 128],
                                   bfc[:, (z % 2) * 2048 + 1024 + g * 128:
                                       (z % 2) * 2048 + 1024 + (g + 1) * 128],
                                   start=True, stop=True).then_inc(S["b2sc"], 1)
                        if zg >= 1:
                            eng.wait_ge(S["b2ea"], zg)
                            eng.wait_ge(S["b2r"], 8 * z + g)
                        eng.wait_ge(S["b2rs"], 16 * (zg + 1))
                        for jj in range(4):
                            eng.matmul(pb[jj][:], colvst[:, g * 128:(g + 1) * 128],
                                       bi16[:, jj * 512:(jj + 1) * 512],
                                       start=True, stop=False)
                            eng.matmul(pb[jj][:], k_ones[0:3, 0:128],
                                       rowsa_t[zg % 2][0:3, jj * 512:(jj + 1) * 512],
                                       start=False, stop=False)
                            mm = eng.matmul(pb[jj][:].rearrange("p (a l) -> p a l",
                                                                a=4),
                                            k_sl,
                                            k_negid[:, None, :]
                                            .broadcast_to([128, 4, 128]),
                                            start=False, stop=True)
                        mm.then_inc(S["b2d"], 1)
                        eng.wait_ge(S["b2L"], zg + 1)
                        for jj in range(4):
                            mm = eng.matmul(pb[jj][:], k_ones[0:3, 0:128],
                                            rowsa_t[zg % 2][0:3,
                                                            jj * 512:(jj + 1) * 512],
                                            start=True, stop=True)
                        mm.then_inc(S["b2eb"], 1)
                        eng.wait_ge(S["b2G"], zg + 1)
                        eng.wait_ge(S["b2ce"], zg + 1)
                        eng.wait_ge(S["b2ye"], 16 * z + 2 * g)
                        if z >= 1:
                            eng.wait_ge(S["b2r"], 8 * (z - 1) + g + 1)
                        for hh in range(16):
                            h = g * 16 + hh
                            bank = pb[4 + hh // 8]
                            base = (hh % 2) * 64
                            slot = (hh // 2) % 4
                            eng.matmul(bank[base:base + 64,
                                            slot * 128:(slot + 1) * 128],
                                       hsT[z % 2][:, h * 64:(h + 1) * 64],
                                       l_t[zg % 2][:, hh * 128:(hh + 1) * 128],
                                       start=True, stop=False)
                            mm = eng.matmul(bank[base:base + 64,
                                                 slot * 128:(slot + 1) * 128],
                                            r_f[:, h * 64:(h + 1) * 64],
                                            ce_t[zg % 2][:, hh * 128:(hh + 1) * 128],
                                            start=False, stop=True)
                        mm.then_inc(S["b2y"], 1)
                        eng.wait_ge(S["b2ea"], zg + 1)
                        eng.wait_ge(S["b2x"], z + 1)
                        b_states_mm(eng, "b2", z, g, 0)
            elif role == "scalar":
                for z in range(NZ):
                    b_tr_evac(eng, "b2", z)
                    b_gbc_evac(eng, "b2", z)
                    for g in range(G):
                        zg = 8 * z + g
                        eng.wait_ge(S["b2sc"], zg + 1)
                        if zg >= 2:
                            eng.wait_ge(S["b2G"], zg - 1)
                        eng.activation(ssb[:, (zg % 2) * 128:(zg % 2 + 1) * 128],
                                       pbs[:, 0:128], AF.Copy
                                       ).then_inc(S["b2scv"], 1)
                        eng.wait_ge(S["b2d"], zg + 1)
                        if zg >= 2:
                            eng.wait_ge(S["b2y"], zg - 1)
                        for jj in range(4):
                            ins = eng.activation(
                                l_t[zg % 2][:, jj * 512:(jj + 1) * 512],
                                pb[jj][:], AF.Exp)
                        ins.then_inc(S["b2L"], 1)
                        eng.wait_ge(S["b2eb"], zg + 1)
                        if zg >= 2:
                            eng.wait_ge(S["b2y"], zg - 1)
                        for jj in range(4):
                            ins = eng.activation(
                                ce_t[zg % 2][:, jj * 512:(jj + 1) * 512],
                                pb[jj][:], AF.Exp)
                        ins.then_inc(S["b2ea"], 1)
                        eng.wait_ge(S["b2y"], zg + 1)
                        if z >= 1:
                            eng.wait_ge(S["b2ys"], 16 * z)
                        eng.activation(yst[:, (g * 8) * 128:(g * 8 + 4) * 128],
                                       pb[4][:], AF.Copy)
                        eng.activation(yst[:, (g * 8 + 4) * 128:(g * 8 + 8) * 128],
                                       pb[5][:], AF.Copy).then_inc(S["b2ye"], 2)
            elif role == "vector":
                for z in range(NZ):
                    b_xchd(eng, "b2", z, hsT, 80)
                    for g in range(G):
                        zg = 8 * z + g
                        eng.wait_ge(S["b2L"], zg + 1)
                        eng.wait_ge(S["b2scv"], zg + 1)
                        lv = l_t[zg % 2].rearrange("p (hh l) -> p hh l", hh=16)
                        eng.tensor_tensor(
                            lv, lv,
                            ssb[:, None, (zg % 2) * 128:(zg % 2 + 1) * 128]
                            .broadcast_to([128, 16, 128]),
                            ALU.mult).then_inc(S["b2G"], 1)
                        eng.wait_ge(S["b2ea"], zg + 1)
                        cv_ = ce_t[zg % 2].rearrange("p (hh l) -> p hh l", hh=16)
                        eng.tensor_tensor(
                            cv_, cv_,
                            bfc[:, None, (z % 2) * 2048 + 1024 + g * 128:
                                (z % 2) * 2048 + 1024 + (g + 1) * 128]
                            .broadcast_to([128, 16, 128]),
                            ALU.mult).then_inc(S["b2ce"], 1)
                        b_rupd_g(eng, "b2", z, g, 0, True)

        def c_ph(role, eng):
            if role == "sync":
                eng.wait_ge(S["b2ys"], 16 * NZ)
                for it in range(16):
                    fg, th = it // 2, it % 2
                    if it >= 1:
                        eng.wait_ge(S["cn"], 8 * it)
                    tsl = slice(th * 512, (th + 1) * 512)
                    eng.dma_start(out=c_y.rearrange("p (a t) -> p a t", a=8),
                                  in_=y_scr[fg * 1024:(fg + 1) * 1024, tsl]
                                  .rearrange("(a p) t -> p a t", p=128)
                                  ).then_inc(S["cl"], 16)
                    eng.dma_start(out=c_g.rearrange("p (a t) -> p a t", a=8),
                                  in_=gate_scr[fg * 1024:(fg + 1) * 1024, tsl]
                                  .rearrange("(a p) t -> p a t", p=128)
                                  ).then_inc(S["cl"], 16)
                    eng.dma_start(out=c_h.rearrange("p (a t) -> p a t", a=8),
                                  in_=hbc_scr[fg * 1024:(fg + 1) * 1024, tsl]
                                  .rearrange("(a p) t -> p a t", p=128)
                                  ).then_inc(S["cl"], 16)
            elif role == "scalar":
                for it in range(16):
                    eng.wait_ge(S["cl"], 48 * (it + 1))
                    for a in range(8):
                        eng.activation(c_w[:, a * 512:(a + 1) * 512],
                                       c_g[:, a * 512:(a + 1) * 512], AF.Silu
                                       ).then_inc(S["cs1"], 1)
                        eng.wait_ge(S["cv1"], 16 * it + 2 * a + 2)
                        eng.activation(c_g[:, a * 512:(a + 1) * 512],
                                       c_w[:, a * 512:(a + 1) * 512], AF.Square
                                       ).then_inc(S["cs1"], 1)
                    eng.wait_ge(S["cm"], it + 1)
                    if it >= 1:
                        eng.wait_ge(S["cbc"], it)
                    eng.activation(stat[0:1, 0:512], pbs[0:1, 0:512], AF.Ln,
                                   scale=1.0 / 1024.0, bias=kbias[0:1, 0:1])
                    eng.activation(statb[0:1, 0:512], stat[0:1, 0:512], AF.Exp,
                                   scale=-0.5).then_inc(S["cs1"], 1)
            elif role == "vector":
                for it in range(16):
                    fg, th = it // 2, it % 2
                    for a in range(8):
                        eng.wait_ge(S["cs1"], 17 * it + 2 * a + 1)
                        eng.scalar_tensor_tensor(
                            c_y[:, a * 512:(a + 1) * 512],
                            c_h[:, a * 512:(a + 1) * 512],
                            dxsb[:, fg * 8 + a:fg * 8 + a + 1],
                            c_y[:, a * 512:(a + 1) * 512],
                            ALU.mult, ALU.add).then_inc(S["cv1"], 1)
                        eng.tensor_mul(c_w[:, a * 512:(a + 1) * 512],
                                       c_y[:, a * 512:(a + 1) * 512],
                                       c_w[:, a * 512:(a + 1) * 512]
                                       ).then_inc(S["cv1"], 1)
                    eng.wait_ge(S["cbc"], it + 1)
                    for a in range(8):
                        eng.scalar_tensor_tensor(
                            normed[:, fg * 8 + a, th * 512:(th + 1) * 512],
                            c_w[:, a * 512:(a + 1) * 512],
                            nwsb[:, fg * 8 + a:fg * 8 + a + 1],
                            pb[0][:], ALU.mult, ALU.mult).then_inc(S["cn"], 1)
            elif role == "tensor":
                for it in range(16):
                    for a in range(8):
                        eng.wait_ge(S["cs1"], 17 * it + 2 * a + 2)
                        mm = eng.matmul(pbs[0:1, 0:512], k_ones[:, 0:1],
                                        c_g[:, a * 512:(a + 1) * 512],
                                        start=(a == 0), stop=(a == 7))
                    mm.then_inc(S["cm"], 1)
                    eng.wait_ge(S["cs1"], 17 * (it + 1))
                    if it >= 1:
                        eng.wait_ge(S["cn"], 8 * it)
                    eng.matmul(pb[0][:], k_ones[0:1, :], statb[0:1, 0:512],
                               start=True, stop=True).then_inc(S["cbc"], 1)

        def d_ph(role, eng):
            if role == "gpsimd":
                for idx in range(64):
                    ot, kh = idx // 2, idx % 2
                    if idx >= 3:
                        eng.wait_ge(S["dmm"], (idx - 3) // 2 + 1)
                    eng.dma_start(
                        out=wring[idx % 3].rearrange("p (kb e) -> p kb e", kb=32),
                        in_=WoutT[kh * 4096:(kh + 1) * 4096,
                                  ot * 128:(ot + 1) * 128]
                        .rearrange("(kb p) e -> p kb e", p=128)
                        ).then_inc(S["dw"], 16)
            elif role == "sync":
                eng.wait_ge(S["est"], 16 * 32)
                for ot in range(32):
                    if ot >= 1:
                        eng.wait_ge(S["dev"], ot)
                    eng.dma_start(out=d_xc, in_=xT[ot * 128:(ot + 1) * 128, 3:1027]
                                  ).then_inc(S["dl"], 16)
                    eng.dma_start(out=d_fc, in_=ffn_scr[ot * 128:(ot + 1) * 128, :]
                                  ).then_inc(S["dl"], 16)
                    eng.wait_ge(S["dte"], ot + 1)
                    eng.dma_start(out=o_out.rearrange("(tt t) o -> t tt o", t=128)
                                  [:, :, ot * 128:(ot + 1) * 128],
                                  in_=d_ot.rearrange("p (tt o) -> p tt o", tt=8)
                                  ).then_inc(S["dst"], 16)
                eng.wait_ge(S["dst"], 16 * 32)
            elif role == "tensor":
                for ot in range(32):
                    s2 = 2 * (ot % 2)
                    for kh in range(2):
                        idx = 2 * ot + kh
                        eng.wait_ge(S["dw"], 16 * (idx + 1))
                        if ot >= 2 and kh == 0:
                            eng.wait_ge(S["dev"], ot - 1)
                        wr = wring[idx % 3].rearrange("p (kb e) -> p kb e", kb=32)
                        for k in range(32):
                            j = kh * 32 + k
                            eng.matmul(pb[s2][:], wr[:, k, :], normed[:, j, 0:512],
                                       start=(j == 0), stop=False)
                            mm = eng.matmul(pb[s2 + 1][:], wr[:, k, :],
                                            normed[:, j, 512:1024],
                                            start=(j == 0), stop=(j == 63))
                    mm.then_inc(S["dmm"], 1)
                    eng.wait_ge(S["dev"], ot + 1)
                    if ot >= 1:
                        eng.wait_ge(S["dte"], ot)
                    for tt in range(8):
                        bank = pb[4 + tt // 4].bitcast(F32) if False else pb[4 + tt // 4]
                        mm = eng.transpose(bank[:, (tt % 4) * 128:(tt % 4 + 1) * 128],
                                           d_os[:, tt * 128:(tt + 1) * 128], idf)
                    mm.then_inc(S["dtr"], 1)
            elif role == "vector":
                for ot in range(32):
                    eng.wait_ge(S["dmm"], ot + 1)
                    eng.wait_ge(S["dl"], 32 * (ot + 1))
                    if ot >= 1:
                        eng.wait_ge(S["dtr"], ot)
                    s2 = 2 * (ot % 2)
                    eng.tensor_add(d_os[:, 0:512], pb[s2][:], d_xc[:, 0:512])
                    eng.tensor_add(d_os[:, 512:1024], pb[s2 + 1][:],
                                   d_xc[:, 512:1024])
                    eng.tensor_add(d_os[:, 0:512], d_os[:, 0:512], d_fc[:, 0:512])
                    eng.tensor_add(d_os[:, 512:1024], d_os[:, 512:1024],
                                   d_fc[:, 512:1024]).then_inc(S["dev"], 1)
            elif role == "scalar":
                for ot in range(32):
                    eng.wait_ge(S["dtr"], ot + 1)
                    if ot >= 1:
                        eng.wait_ge(S["dst"], 16 * ot)
                    for tt in range(8):
                        bank = pb[4 + tt // 4]
                        ins = eng.activation(
                            d_ot[:, tt * 128:(tt + 1) * 128],
                            bank[:, (tt % 4) * 128:(tt % 4 + 1) * 128], AF.Copy)
                    ins.then_inc(S["dte"], 1)

        def dumps_ph(role, eng):
            if role != "sync":
                return
            items = []
            for name, ap in dump_aps.items():
                if name == "xn":
                    items.append((ap, ar1[:, 0:32 * TH]))
                elif name == "gate":
                    items.append((ap, gate_scr))
                elif name == "hbc":
                    items.append((ap, hbc_scr))
                elif name == "dtpack":
                    items.append((ap[:, 0:1024], f_dt[:, 0:1024]))
                    items.append((ap[:, 1024:2048], f_ac[:, 0:1024]))
                    items.append((ap[:, 2048:2056], csg[:]))
                elif name == "dtbf":
                    items.append((ap, ard))
                elif name == "vecs":
                    items.append((ap, vecs[:]))
                elif name == "fm":
                    items.append((ap, f_m[:, 0:1024]))
                elif name == "y":
                    items.append((ap, y_scr))
                elif name == "ffn":
                    items.append((ap, ffn_scr))
                elif name == "normed":
                    items.append((ap, ar1[:, 0:65536]))
                elif name == "rstate":
                    items.append((ap, r_f))
                elif name == "axc":
                    items.append((ap, e_axc[0:64, :]))
                elif name == "bcat":
                    items.append((ap, e_bcat[0:64, :]))
            cnt = 0
            for out, src in items:
                eng.dma_start(out=out, in_=src).then_inc(S["dmp"], 16)
                cnt += 1
            if cnt:
                eng.wait_ge(S["dmp"], 16 * cnt)

        ETS2 = [(18432, "dt")] + [(8192 + a * 128, "hbc") for a in range(72)]

        def q0_ph(role, eng):
            p0(role, eng, PS="q0", x_src=xT2)

        def qa_ph(role, eng):
            a_ph(role, eng, PS="q", ets=ETS2, scr=hbc2_scr, full=False,
                 gam=gamr2, dd=ddec2)

        PH_FN = {"p0": p0, "a": a_ph, "e": e_ph, "q0": q0_ph, "qa": qa_ph,
                 "qb": b1, "b2": b2, "c": c_ph, "d": d_ph}
        phases = [PH_FN[p] for p in use]

        def emit(role, eng):
            for i, ph in enumerate(phases):
                ph(role, eng)
                if role == "vector":
                    eng.tensor_copy(kbias[0:1, 0:1], kbias[0:1, 0:1]
                                    ).then_inc(S["bar"], 1)
                elif role == "scalar":
                    eng.activation(kbias[0:1, 1:2], kbias[0:1, 1:2], AF.Copy
                                   ).then_inc(S["bar"], 1)
                elif role == "gpsimd":
                    eng.memset(zrf[0:1, 0:1], 0.0).then_inc(S["bar"], 1)
                else:
                    eng.sem_inc(S["bar"], 1)
                eng.wait_ge(S["bar"], 5 * (i + 1))
            if dump_aps:
                dumps_ph(role, eng)

        @blk.sync
        def _(eng):
            emit("sync", eng)

        @blk.gpsimd
        def _(eng):
            emit("gpsimd", eng)

        @blk.tensor
        def _(eng):
            emit("tensor", eng)

        @blk.vector
        def _(eng):
            emit("vector", eng)

        @blk.scalar
        def _(eng):
            emit("scalar", eng)

    return nc


def _prep_inputs(inputs):
    f32 = np.float32
    bf = ml_dtypes.bfloat16
    x = np.asarray(inputs["L0_attn_x"], f32)
    WinT = np.ascontiguousarray(np.asarray(inputs["L0_attn_in_proj_w"], f32).T)
    WoutT = np.ascontiguousarray(np.asarray(inputs["L0_attn_out_proj_w"], f32).T)
    kern = np.ascontiguousarray(np.asarray(inputs["L0_ffn_kernel"], f32))
    lora_A = np.asarray(inputs["L0_ffn_lora_A"], f32)
    lora_B = np.asarray(inputs["L0_ffn_lora_B"], f32)
    loraAt = np.ascontiguousarray(lora_A.transpose(0, 2, 1))
    loraAt4 = np.ascontiguousarray(
        loraAt.reshape(4, 4, D, 32).transpose(0, 2, 1, 3).reshape(4, D, 128))
    loraBperm = np.ascontiguousarray(lora_B.transpose(0, 2, 1).reshape(512, D))
    loraAst = np.ascontiguousarray(np.asarray(inputs["L0_ffn_lora_A_shared"], f32).T)
    loraBst = np.ascontiguousarray(np.asarray(inputs["L0_ffn_lora_B_shared"], f32).T)
    gate = np.asarray(inputs["L0_ffn_gate"], f32)
    conv_w4 = np.ascontiguousarray(
        np.asarray(inputs["L0_attn_conv_w"], f32).reshape(80, 128, 4)
        .transpose(1, 0, 2).reshape(128, 320))
    conv_b = np.ascontiguousarray(np.asarray(inputs["L0_attn_conv_b"], f32)
                                  .reshape(80, 128).T)
    dtb = np.ascontiguousarray(np.asarray(inputs["L0_attn_dt_bias"], f32)
                               .reshape(NH, 1))
    Alog = np.ascontiguousarray(np.asarray(inputs["L0_attn_A_log"], f32)
                                .reshape(NH, 1))
    Dexp = np.ascontiguousarray(
        np.repeat(np.asarray(inputs["L0_attn_D"], f32), PH).reshape(64, 128).T)
    nwcol = np.ascontiguousarray(np.asarray(inputs["L0_attn_norm_w"], f32)
                                 .reshape(64, 128).T)
    bias_col = np.ascontiguousarray(np.asarray(inputs["L0_ffn_bias"], f32)
                                    .reshape(32, 128).T)

    eye = np.eye(128, dtype=f32)
    c_ident = eye.astype(bf)
    c_identf = eye.copy()
    c_sl = (np.arange(128)[:, None] < np.arange(128)[None, :]).astype(f32).astype(bf)
    c_negid = (NEG_BIG * eye).astype(bf)
    bi = np.zeros((16, 2048), f32)
    for j in range(16):
        bi[j, j * 128:(j + 1) * 128] = 1.0
    c_bi16 = np.ascontiguousarray(np.tile(bi, (3, 1))).astype(bf)
    c_ones = np.ones((128, 128), f32).astype(bf)
    c_id4 = np.tile(np.eye(32, dtype=f32), (4, 1)).astype(bf)
    selgm = np.zeros((16, 512), f32)
    for grp in range(4):
        for p in range(128):
            selgm[grp * 4 + p // 32, grp * 128 + p] = 1.0
    c_selg = selgm.astype(bf)

    shared = dict(WinT=WinT, WoutT=WoutT, kern=kern, loraAt4=loraAt4,
                  loraBperm=loraBperm, loraAst=loraAst, loraBst=loraBst,
                  conv_w4=conv_w4, conv_b=conv_b, dtb=dtb, Alog=Alog,
                  Dexp=Dexp, nwcol=nwcol, bias_col=bias_col,
                  c_ident=c_ident, c_identf=c_identf, c_sl=c_sl,
                  c_negid=c_negid, c_bi16=c_bi16, c_ones=c_ones,
                  c_id4=c_id4, c_selg=c_selg)

    in_maps = []
    for c in range(8):
        b, half = c // 2, c % 2
        if half == 0:
            xs = np.concatenate([np.zeros((3, D), f32), x[b, 0:T]], axis=0)
        else:
            xs = x[b, T - 3:2 * T]
        m = dict(shared)
        m["xT"] = np.ascontiguousarray(xs.T)
        xs2 = np.concatenate([np.zeros((3, D), f32), x[b, 0:T]], axis=0)
        m["xT2"] = np.ascontiguousarray(xs2.T)
        m["gate_col"] = np.ascontiguousarray(gate[c].reshape(16, 1))
        m["cmask"] = np.full((NH, 1), 1.0 - half, f32)
        m["umask"] = np.full((NH, 1), float(half), f32)
        in_maps.append(m)
    return in_maps


def kernel(**inputs):
    in_maps = _prep_inputs(inputs)
    nc = build_program()
    res = run_bass_kernel_spmd(nc, in_maps, core_ids=list(range(8)))
    out = np.empty((4, 2048, D), np.float32)
    for c in range(8):
        b, half = c // 2, c % 2
        out[b, half * T:(half + 1) * T] = res.results[c]["o_out"]
    return out
